# version 02
# baseline (speedup 1.0000x reference)
import contextlib
import numpy as np
import concourse.bass as bass
import concourse.mybir as mybir
from concourse.bass_utils import run_bass_kernel_spmd

F32 = mybir.dt.float32
F16 = mybir.dt.float16
AF = mybir.ActivationFunctionType
ALU = mybir.AluOpType
AX = mybir.AxisListType.X

NCORES = 8
SEQS = 2
N = 4096
D = 1024
CH = 8
T = 512
NT = N // T
NB = T // 128
NH = 16
WS = 8
E = 8
EPS_LN = 1e-5
EPS_RMS = 1e-6

XR_C = 524
QK_C = 520
CT_C = 520
S1_C = 514

ENGINES = ("tensor", "vector", "scalar", "sync", "gpsimd")

DEBUG_TAPS = set()
TAP_S = 0
TAP_T = 0
LIMIT_TILES = None
UPTO = 99
ATTN_PARTS = None
ATTN_SUB = 99


class Prog:

    def __init__(self, nc):
        self.nc = nc
        self.ops = {e: [] for e in ENGINES}
        self.sems = {}
        self.cnt = {}
        self.trace = {e: [] for e in ENGINES}

    def add_sem(self, name, handle):
        self.sems[name] = handle
        self.cnt[name] = 0

    def op(self, eng, fn, inc=None, n=1):
        if inc is not None:
            self.cnt[inc] += n
            sem = self.sems[inc]

            def run(e, fn=fn, sem=sem, n=n):
                fn(e).then_inc(sem, n)
        else:
            def run(e, fn=fn):
                fn(e)
        self.trace[eng].append(("inc", inc, n) if inc else ("inc", None, 0))
        self.ops[eng].append(run)
        return self.cnt[inc] if inc is not None else None

    def wait(self, eng, sem, val):
        assert val <= self.cnt[sem], f"wait {sem}>={val} but only {self.cnt[sem]} emitted"
        if val <= 0:
            return
        s = self.sems[sem]
        self.trace[eng].append(("wait", sem, val))
        self.ops[eng].append(lambda e, s=s, v=val: e.wait_ge(s, v))

    def check_deadlock(self):
        pos = {e: 0 for e in ENGINES}
        cur = {s: 0 for s in self.cnt}
        progress = True
        while progress:
            progress = False
            for e in ENGINES:
                tr = self.trace[e]
                while pos[e] < len(tr):
                    kind, sem, v = tr[pos[e]]
                    if kind == "wait" and cur[sem] < v:
                        break
                    if kind == "inc" and sem is not None:
                        cur[sem] += v
                    pos[e] += 1
                    progress = True
        stuck = {e: (pos[e], len(self.trace[e])) for e in ENGINES
                 if pos[e] < len(self.trace[e])}
        if stuck:
            detail = {e: self.trace[e][pos[e]] for e in stuck}
            raise RuntimeError(f"deadlock: stuck engines {stuck} at {detail}")

    def mark(self, sem):
        return self.cnt[sem]

    def replay(self, block):
        for eng in ENGINES:
            ops = self.ops[eng]
            if not ops:
                continue

            def body(handle, ops=ops):
                for f in ops:
                    f(handle)
            getattr(block, eng)(body)


class Bank:

    def __init__(self, prog, ap):
        self.P = prog
        self.ap = ap
        self.wmark = 0
        self.readers = []

    def acquire(self):
        for sem, val in self.readers:
            self.P.wait("tensor", sem, val)
        self.readers = []

    def written(self):
        self.wmark = self.P.mark("pe")

    def rwait(self, eng):
        self.P.wait(eng, "pe", self.wmark)

    def radd(self, sem):
        self.readers.append((sem, self.P.mark(sem)))


class WStream:

    def __init__(self, prog, slots, semnames):
        self.P = prog
        self.slots = slots
        self.sems = semnames
        self.q = []
        self.issued = []
        self.guard = [0, 0]
        self.nxt = 0

    def push(self, scr_ap, g):
        self.q.append((scr_ap, g))

    def _issue_one(self):
        scr_ap, g = self.q.pop(0)
        s = self.nxt
        self.nxt ^= 1
        self.P.wait("sync", "pe", self.guard[s])
        dst = self.slots[s][:].rearrange("p (hl f) -> p hl f", hl=2)
        src = scr_ap[g].rearrange("hl p f -> p hl f")
        self.P.op("sync", lambda sy, d=dst, sr=src: sy.dma_start(out=d, in_=sr),
                  inc=self.sems[s], n=16)
        self.issued.append((s, self.P.mark(self.sems[s])))

    def prime(self):
        while len(self.issued) < 2 and self.q:
            self._issue_one()

    def consume(self):
        s, target = self.issued.pop(0)
        self.P.wait("tensor", self.sems[s], target)
        return s

    def block(self, s, hl, m_local, k, nk):
        i = 2048 * hl + (m_local * nk + k) * 128
        return self.slots[s][:, i:i + 128]

    def done(self, s):
        self.guard[s] = self.P.mark("pe")
        if self.q:
            self._issue_one()


def build_nc():
    nc = bass.Bass("TRN2", target_bir_lowering=False)

    xin = nc.dram_tensor("xin", [SEQS, N, D], F32, kind="ExternalInput").ap()
    y = nc.dram_tensor("y", [SEQS, N, D], F32, kind="ExternalOutput").ap()

    wqkv_b = nc.dram_tensor("wqkv_b", [24, 8, 128, 128], F32, kind="ExternalInput").ap()
    wproj_b = nc.dram_tensor("wproj_b", [8, 8, 128, 128], F32, kind="ExternalInput").ap()
    wpw_b = nc.dram_tensor("wpw_b", [8, 8, 128, 128], F32, kind="ExternalInput").ap()
    wexp_b = nc.dram_tensor("wexp_b", [64, 8, 128, 128], F32, kind="ExternalInput").ap()
    wout_b = nc.dram_tensor("wout_b", [8, 8, 128, 128], F32, kind="ExternalInput").ap()
    wffd_b = nc.dram_tensor("wffd_b", [2, 8, 128, 128], F32, kind="ExternalInput").ap()
    wffu_b = nc.dram_tensor("wffu_b", [8, 2, 128, 128], F32, kind="ExternalInput").ap()

    KINDS = [
        ("qkv", wqkv_b, 12, 2, 8),
        ("proj", wproj_b, 4, 2, 8),
        ("pw", wpw_b, 4, 2, 8),
        ("exp", wexp_b, 32, 2, 8),
        ("out", wout_b, 4, 2, 8),
        ("ffd", wffd_b, 1, 2, 8),
        ("ffu", wffu_b, 1, 8, 2),
    ]
    scr = {}
    for nm, _, ng, _, _ in KINDS:
        scr[nm] = nc.dram_tensor("scr_" + nm, [ng, 2, 128, 2048], F16).ap()

    mask_c = nc.dram_tensor("mask_c", [128, NH * 128], F32, kind="ExternalInput").ap()
    ident_c = nc.dram_tensor("ident_c", [128, 128], F32, kind="ExternalInput").ap()
    onescol_c = nc.dram_tensor("onescol_c", [128, 1], F32, kind="ExternalInput").ap()
    onesrow_c = nc.dram_tensor("onesrow_c", [1, 128], F32, kind="ExternalInput").ap()
    sel_c = nc.dram_tensor("sel_c", [8, 8 * 128], F32, kind="ExternalInput").ap()
    bqkv_c = nc.dram_tensor("bqkv_c", [128, 24], F32, kind="ExternalInput").ap()
    bproj_c = nc.dram_tensor("bproj_c", [128, 8], F32, kind="ExternalInput").ap()
    pwb_c = nc.dram_tensor("pwb_c", [128, 8], F32, kind="ExternalInput").ap()
    outb_c = nc.dram_tensor("outb_c", [128, 8], F32, kind="ExternalInput").ap()
    ffub_c = nc.dram_tensor("ffub_c", [128, 8], F32, kind="ExternalInput").ap()
    ffdb_c = nc.dram_tensor("ffdb_c", [128, 2], F32, kind="ExternalInput").ap()
    nw_c = nc.dram_tensor("nw_c", [128, 8], F32, kind="ExternalInput").ap()
    dwb_c = nc.dram_tensor("dwb_c", [128, 8], F32, kind="ExternalInput").ap()
    dwk_c = nc.dram_tensor("dwk_c", [128, 24], F32, kind="ExternalInput").ap()
    rt_c = nc.dram_tensor("rt_c", [128, 64], F32, kind="ExternalInput").ap()
    rbbc_c = nc.dram_tensor("rbbc_c", [128, 8], F32, kind="ExternalInput").ap()
    eb_c = nc.dram_tensor("eb_c", [128, 64], F32, kind="ExternalInput").ap()

    taps = {}
    for tag, w in [("xrf", CH * XR_C), ("qf", CH * QK_C), ("kf", CH * QK_C),
                   ("vf", CH * QK_C), ("ctx", CH * CT_C), ("rf", CH * QK_C),
                   ("s1", CH * S1_C), ("cv", CH * QK_C), ("x1", CH * XR_C),
                   ("lg", 32), ("rw", 32), ("comb", CH * 512), ("hp", CH * XR_C),
                   ("h", CH * QK_C), ("ff1", 2 * 512), ("yp", CH * XR_C),
                   ("yf", CH * QK_C)]:
        if tag in DEBUG_TAPS:
            taps[tag] = nc.dram_tensor("dbg_" + tag, [128, w], F32,
                                       kind="ExternalOutput").ap()

    with contextlib.ExitStack() as ctx:
        def sbuf(name, shape):
            return ctx.enter_context(nc.sbuf_tensor(name, shape, F32))

        def sbuf16(name, shape):
            return ctx.enter_context(nc.sbuf_tensor(name, shape, F16))

        XS = sbuf("XS", [128, 4 * 1024])
        XRF = sbuf("XRF", [128, CH * XR_C])
        VF = sbuf("VF", [128, CH * QK_C])
        COMB = sbuf("COMB", [128, CH * 512])
        S1B = sbuf("S1B", [128, CH * S1_C])
        P0 = sbuf("P0", [128, NH * 128])
        VTY = P0[:, 0:1024]
        VT = sbuf("VT", [128, 1024])
        SC = sbuf("SC", [128, 2 * 512])
        FF32 = SC
        XRFh = sbuf16("XRFh", [128, CH * XR_C])
        XRFl = sbuf16("XRFl", [128, CH * XR_C])
        QFh = sbuf16("QFh", [128, CH * QK_C])
        QFl = sbuf16("QFl", [128, CH * QK_C])
        KFh = sbuf16("KFh", [128, CH * QK_C])
        KFl = sbuf16("KFl", [128, CH * QK_C])
        CTXFh = sbuf16("CTXFh", [128, CH * CT_C])
        CTXFl = sbuf16("CTXFl", [128, CH * CT_C])
        PTSh = XRFh[:, 0:NH * 128]
        PTSl = XRFl[:, 0:NH * 128]
        VTh = sbuf16("VTh", [128, 1024])
        VTl = sbuf16("VTl", [128, 1024])
        CTAILh = sbuf16("CTAILh", [128, CH * 8])
        CTAILl = sbuf16("CTAILl", [128, CH * 8])
        LG = sbuf("LG", [128, 32])
        SM = sbuf("SM", [128, 64])
        RWV = sbuf("RWV", [128, 32])
        RWF8 = sbuf("RWF8", [8, 512])
        SEL = sbuf("SEL", [8, 8 * 128])
        STATS = sbuf("STATS", [1, 4 * 512])
        NS = sbuf("NS", [128, 48])
        MSK = sbuf("MSK", [128, NH * 128])
        IDN = sbuf("IDN", [128, 128])
        OC = sbuf("OC", [128, 1])
        OR = sbuf("OR", [1, 128])
        BQ = sbuf("BQ", [128, 24])
        BP = sbuf("BP", [128, 8])
        PWB = sbuf("PWB", [128, 8])
        OB = sbuf("OB", [128, 8])
        FUB = sbuf("FUB", [128, 8])
        FDB = sbuf("FDB", [128, 2])
        NW = sbuf("NW", [128, 8])
        DWB = sbuf("DWB", [128, 8])
        DWK = sbuf("DWK", [128, 24])
        RT = sbuf("RT", [128, 64])
        RBB = sbuf("RBB", [128, 8])
        EB = sbuf("EB", [128, 64])
        WA = sbuf16("WA", [128, 4096])
        WB = sbuf16("WB", [128, 4096])
        W32A = COMB[:, 0:2048]
        W32B = P0[:, 0:2048]

        pb = [ctx.enter_context(nc.psum_tensor(f"pb{i}", [128, 512], F32))
              for i in range(8)]

        P = Prog(nc)
        for sname in ("pe", "dve", "act", "gp", "dwa", "dwb", "dx", "dy0",
                      "dy1", "dc", "ddbg", "s32a", "s32b", "dscr"):
            P.add_sem(sname, ctx.enter_context(nc.semaphore(sname)))
        banks = [Bank(P, t) for t in pb]
        ws = WStream(P, [WA, WB], ["dwa", "dwb"])

        def vop(fn):
            return P.op("vector", fn, inc="dve")

        def aop(fn):
            return P.op("scalar", fn, inc="act")

        def mm(out_ap, lhs_ap, rhs_ap, start, stop, inc=False, tile_position=None):
            def f(t, o=out_ap, l=lhs_ap, r=rhs_ap, st=start, sp=stop, tp=tile_position):
                if tp is not None:
                    return t.matmul(o, l, r, start=st, stop=sp, tile_position=tp)
                return t.matmul(o, l, r, start=st, stop=sp)
            return P.op("tensor", f, inc="pe" if inc else None)

        def tp(out_ap, in_ap, id_ap, inc=False):
            return P.op("tensor",
                        lambda t, o=out_ap, i=in_ap, d=id_ap: t.transpose(o, i, d),
                        inc="pe" if inc else None)

        def mm3(bank_ap, blkfn, rh, rl, nk, tpos=None):
            i, n = 0, 3 * nk
            for hl, rf in ((0, rh), (1, rh), (0, rl)):
                for k in range(nk):
                    mm(bank_ap, blkfn(hl, k), rf(k),
                       start=(i == 0), stop=(i == n - 1), inc=(i == n - 1),
                       tile_position=tpos)
                    i += 1

        def sp_evac(src32, dh, dl):
            vop(lambda v, s_=src32, d_=dh: v.tensor_copy(d_, s_))
            P.wait("vector", "dve", P.mark("dve"))
            vop(lambda v, s_=src32, h_=dh, d_=dl:
                v.tensor_tensor(d_, s_, h_, ALU.subtract))

        def tap(tag, src_ap, prod_sem, prod_val):
            if tag not in taps:
                return
            P.wait("gpsimd", prod_sem, prod_val)
            P.op("gpsimd", lambda g, o=taps[tag], s=src_ap: g.dma_start(out=o, in_=s),
                 inc="ddbg", n=16)

        def cload(dst, src):
            P.op("sync", lambda sy, d=dst, s=src: sy.dma_start(out=d, in_=s),
                 inc="dc", n=16)

        cload(MSK[:], mask_c)
        cload(IDN[:], ident_c)
        cload(OC[:], onescol_c)
        cload(OR[:], onesrow_c)
        cload(SEL[:], sel_c)
        cload(BQ[:], bqkv_c)
        cload(BP[:], bproj_c)
        cload(PWB[:], pwb_c)
        cload(OB[:], outb_c)
        cload(FUB[:], ffub_c)
        cload(FDB[:], ffdb_c)
        cload(NW[:], nw_c)
        cload(DWB[:], dwb_c)
        cload(DWK[:], dwk_c)
        cload(RT[:], rt_c)
        cload(RBB[:], rbbc_c)
        cload(EB[:], eb_c)
        dc_total = P.mark("dc")
        for eng in ENGINES:
            P.wait(eng, "dc", dc_total)

        w32 = [W32A, W32B]
        s32 = ["s32a", "s32b"]
        out_guard = [0, 0]
        dve_guard = [0, 0]
        i = 0
        for nm, wb, ng, gm, nk in KINDS:
            for g in range(ng):
                j = i % 2
                P.wait("sync", "dve", dve_guard[j])
                src = wb[gm * g:gm * (g + 1)].rearrange("m k p f -> p m k f")
                dst = w32[j][:].rearrange("p (m k f) -> p m k f", m=gm, k=nk)
                P.op("sync", lambda sy, d=dst, sr=src: sy.dma_start(out=d, in_=sr),
                     inc=s32[j], n=16)
                P.wait("vector", s32[j], P.mark(s32[j]))
                P.wait("vector", "dscr", out_guard[j])
                slot = [WA, WB][j]
                vop(lambda v, s_=w32[j], d_=slot: v.tensor_copy(d_[:, 0:2048], s_))
                P.wait("vector", "dve", P.mark("dve"))
                vop(lambda v, s_=w32[j], d_=slot:
                    v.tensor_tensor(d_[:, 2048:4096], s_, d_[:, 0:2048], ALU.subtract))
                dve_guard[j] = P.mark("dve")
                P.wait("sync", "dve", dve_guard[j])
                sdst = scr[nm][g].rearrange("hl p f -> p hl f")
                ssrc = slot[:].rearrange("p (hl f) -> p hl f", hl=2)
                P.op("sync", lambda sy, d=sdst, sr=ssrc: sy.dma_start(out=d, in_=sr),
                     inc="dscr", n=16)
                out_guard[j] = P.mark("dscr")
                i += 1
        P.wait("sync", "dscr", P.mark("dscr"))
        P.wait("sync", "dve", P.mark("dve"))
        P.wait("gpsimd", "dve", P.mark("dve"))

        st = dict(xs_guard=0, ctail_mark=0, ctxf_free=0,
                  ytm_inflight=[0, 0], xrf_mark=0)

        def emit_tile(s, t):
            base = T * t

            for g in range(12):
                ws.push(scr["qkv"], g)
            for g in range(4):
                ws.push(scr["proj"], g)
            for g in range(4):
                ws.push(scr["pw"], g)
            for g in range(32):
                ws.push(scr["exp"], g)
            for g in range(4):
                ws.push(scr["out"], g)
            ws.push(scr["ffd"], 0)
            ws.push(scr["ffu"], 0)
            ws.prime()

            P.wait("gpsimd", "pe", st["xs_guard"])

            def xload(cols, tok0, ntok, poff=0):
                P.op("gpsimd",
                     lambda g, c=cols, t0=tok0, nt_=ntok, po=poff, s_=s:
                     g.dma_start(out=XS[po:po + nt_, c:c + 1024],
                                 in_=xin[s_, t0:t0 + nt_, :]),
                     inc="dx", n=16)

            for c in range(4):
                xload(1024 * c, base + 128 * c, 128)
            P.wait("gpsimd", "dy0", st["ytm_inflight"][0])
            t4 = (base + 512) % N
            P.op("gpsimd",
                 lambda g, t0=t4, s_=s: g.dma_start(out=VTY[0:4, :],
                                                    in_=xin[s_, t0:t0 + 4, :]),
                 inc="dx", n=16)
            dx_target = P.mark("dx")

            P.wait("tensor", "dx", dx_target)

            def xpose_group(cols_in, np_, dstcol, bank_i, src_t=None):
                src_t = XS if src_t is None else src_t
                for g in range(2):
                    bk = banks[bank_i + g]
                    bk.acquire()
                    for dd in range(4):
                        d = 4 * g + dd
                        tp(pb[bank_i + g][0:128, dd * 128:dd * 128 + np_],
                           src_t[0:np_, cols_in + 128 * d:cols_in + 128 * (d + 1)],
                           IDN[0:np_, 0:np_], inc=(dd == 3))
                    bk.written()
                    bk.rwait("vector")
                    dsl = XRF[:].rearrange("p (c f) -> p c f", c=CH)[:, 4 * g:4 * g + 4,
                                                                    dstcol:dstcol + np_]
                    ssl = pb[bank_i + g][:].rearrange("p (c f) -> p c f", c=4)[:, :, 0:np_]
                    vop(lambda v, d_=dsl, s_=ssl: v.tensor_copy(d_, s_))
                    bk.radd("dve")

            for c in range(4):
                xpose_group(1024 * c, 128, 128 * c, (2 * c) % 8)
            xpose_group(0, 4, 512, 0, src_t=VTY)
            if t == 0:
                P.wait("gpsimd", "pe", P.mark("pe"))
                xload(0, N - 4, 4, poff=0)
                xload(0, 0, 4, poff=4)
                P.wait("tensor", "dx", P.mark("dx"))
                xpose_group(0, 8, 516, 2)
            st["xs_guard"] = P.mark("pe")
            st["xrf_mark"] = P.mark("dve")
            tap("xrf", XRF[:], "dve", st["xrf_mark"])
            if UPTO <= 1:
                return

            P.wait("vector", "dve", st["xrf_mark"])
            wwid = 524 if t == 0 else 516
            for c in range(CH):
                sp_evac(XRF[:, XR_C * c:XR_C * c + wwid],
                        XRFh[:, XR_C * c:XR_C * c + wwid],
                        XRFl[:, XR_C * c:XR_C * c + wwid])
            xrhl_mark = P.mark("dve")
            P.wait("tensor", "dve", xrhl_mark)
            slot = None
            for m in range(24):
                if m % 2 == 0:
                    if slot is not None:
                        ws.done(slot)
                    slot = ws.consume()
                if m < 8:
                    dsth = QFh[:, QK_C * m:QK_C * m + 520]
                    dstl = QFl[:, QK_C * m:QK_C * m + 520]
                    ev = lambda v, d, p, c: v.tensor_scalar(
                        d, p, BQ[:, c:c + 1], 0.125, ALU.add, ALU.mult)
                elif m < 16:
                    dsth = KFh[:, QK_C * (m - 8):QK_C * (m - 8) + 520]
                    dstl = KFl[:, QK_C * (m - 8):QK_C * (m - 8) + 520]
                    ev = lambda v, d, p, c: v.tensor_scalar(
                        d, p, BQ[:, c:c + 1], None, ALU.add)
                else:
                    dsth = None
                    dstl = VF[:, QK_C * (m - 16):QK_C * (m - 16) + 520]
                    ev = lambda v, d, p, c: v.tensor_scalar(
                        d, p, BQ[:, c:c + 1], None, ALU.add)
                bk = banks[m % 4]
                bk.acquire()
                mm3(pb[m % 4][:], lambda hl, k, s_=slot: ws.block(s_, hl, m % 2, k, 8),
                    lambda k: XRFh[:, XR_C * k + 4:XR_C * k + 516],
                    lambda k: XRFl[:, XR_C * k + 4:XR_C * k + 516], CH)
                bk.written()
                bk.rwait("vector")
                if m >= 16:
                    vop(lambda v, d=dstl[:, 0:512], p=pb[m % 4], c=m, e_=ev:
                        e_(v, d, p[:], c))
                else:
                    sc_h = SC[:, 512 * (m % 2):512 * (m % 2) + 512]
                    vop(lambda v, d=sc_h, p=pb[m % 4], c=m, e_=ev: e_(v, d, p[:], c))
                    P.wait("vector", "dve", P.mark("dve"))
                    sp_evac(sc_h, dsth[:, 0:512], dstl[:, 0:512])
                bk.radd("dve")
                if t == 0:
                    bkw = banks[4 + m % 2]
                    bkw.acquire()
                    mm3(pb[4 + m % 2][0:128, 0:8],
                        lambda hl, k, s_=slot: ws.block(s_, hl, m % 2, k, 8),
                        lambda k: XRFh[:, XR_C * k + 516:XR_C * k + 524],
                        lambda k: XRFl[:, XR_C * k + 516:XR_C * k + 524], CH)
                    bkw.written()
                    bkw.rwait("vector")
                    if m >= 16:
                        vop(lambda v, d=dstl[:, 512:520], p=pb[4 + m % 2], c=m, e_=ev:
                            e_(v, d, p[0:128, 0:8], c))
                    else:
                        scw = SM[:, 56:64]
                        vop(lambda v, d=scw, p=pb[4 + m % 2], c=m, e_=ev:
                            e_(v, d, p[0:128, 0:8], c))
                        P.wait("vector", "dve", P.mark("dve"))
                        sp_evac(scw, dsth[:, 512:520], dstl[:, 512:520])
                    bkw.radd("dve")
            if slot is not None:
                ws.done(slot)

            qkv_mark = P.mark("dve")
            if UPTO <= 2:
                return
            tap("qf", QFh[:], "dve", qkv_mark)
            tap("kf", KFh[:], "dve", qkv_mark)
            tap("vf", VF[:], "dve", qkv_mark)

            P.wait("tensor", "dve", qkv_mark)

            def attn_block(qcol, ctx0, W):
                idw = IDN[0:W, 0:W]
                for g in range(2):
                    bk = banks[6 + g]
                    bk.acquire()
                    for dd in range(4):
                        d = 4 * g + dd
                        tp(pb[6 + g][0:W, dd * 128:(dd + 1) * 128],
                           VF[:, QK_C * d + qcol:QK_C * d + qcol + W],
                           IDN[:, :], inc=(dd == 3))
                    bk.written()
                    bk.rwait("vector")
                    sp_evac(pb[6 + g][0:W, :], VTh[0:W, 512 * g:512 * (g + 1)],
                            VTl[0:W, 512 * g:512 * (g + 1)])
                    bk.radd("dve")
                vt_mark = P.mark("dve")
                if ATTN_SUB <= 1:
                    return
                sbank = lambda h: 2 * (h % 2) + (h // 8)
                squar = lambda h: (h // 2) % 4
                for b_ in range(4):
                    banks[b_].acquire()
                for h in range(NH):
                    pp = 64 * (h % 2)
                    tpos = (64, 0) if h % 2 else None
                    qsl = lambda T_, h_=h: T_[64 * (h_ % 2):64 * (h_ % 2) + 64,
                                              QK_C * (h_ // 2) + qcol:
                                              QK_C * (h_ // 2) + qcol + W]
                    for i3, (lh, rh) in enumerate(((QFh, KFh), (QFl, KFh), (QFh, KFl))):
                        mm(pb[sbank(h)][0:W, squar(h) * 128:squar(h) * 128 + W],
                           qsl(lh), qsl(rh), start=(i3 == 0), stop=(i3 == 2),
                           inc=(h == NH - 1 and i3 == 2), tile_position=tpos)
                for b_ in range(4):
                    banks[b_].written()
                if ATTN_SUB <= 2:
                    return
                for h in range(NH):
                    banks[sbank(h)].rwait("vector")
                    vop(lambda v, h_=h: v.tensor_tensor(
                        P0[0:W, 128 * h_:128 * h_ + W],
                        pb[sbank(h_)][0:W, squar(h_) * 128:squar(h_) * 128 + W],
                        MSK[0:W, 128 * h_:128 * h_ + W], ALU.add))
                    banks[sbank(h)].radd("dve")
                m1 = P.mark("dve")
                P.wait("vector", "dve", m1)
                for h in range(NH):
                    vop(lambda v, h_=h: v.tensor_reduce(
                        NS[0:W, h_:h_ + 1], P0[0:W, 128 * h_:128 * h_ + W],
                        axis=AX, op=ALU.max, negate=True))
                m2 = P.mark("dve")
                P.wait("scalar", "dve", m2)
                for h in range(NH):
                    aop(lambda sc, h_=h: sc.activation(
                        P0[0:W, 128 * h_:128 * h_ + W], P0[0:W, 128 * h_:128 * h_ + W],
                        AF.Exp, bias=NS[0:W, h_:h_ + 1],
                        accum_out=NS[0:W, 16 + h_:17 + h_]))
                m3 = P.mark("act")
                P.wait("vector", "act", m3)
                vop(lambda v: v.reciprocal(NS[0:W, 32:48], NS[0:W, 16:32]))
                rinv_mark = P.mark("dve")
                if ATTN_SUB <= 3:
                    return
                P.wait("tensor", "act", m3)
                for h in range(NH):
                    bk = banks[4 + h // 4]
                    if h % 4 == 0:
                        bk.acquire()
                    tp(pb[4 + h // 4][0:W, (h % 4) * 128:(h % 4) * 128 + W],
                       P0[0:W, 128 * h:128 * h + W], idw, inc=(h % 4 == 3))
                    if h % 4 == 3:
                        bk.written()
                for g in range(4):
                    banks[4 + g].rwait("vector")
                    sp_evac(pb[4 + g][0:W, :], PTSh[0:W, 512 * g:512 * (g + 1)],
                            PTSl[0:W, 512 * g:512 * (g + 1)])
                    banks[4 + g].radd("dve")
                pts_mark = P.mark("dve")
                if ATTN_SUB <= 4:
                    return
                P.wait("tensor", "dve", pts_mark)
                for h in range(NH):
                    bk = banks[h // 8]
                    if h % 8 == 0:
                        bk.acquire()
                    for i3, (lh, rh) in enumerate(((PTSh, VTh), (PTSl, VTh), (PTSh, VTl))):
                        mm(pb[h // 8][0:W, 64 * (h % 8):64 * (h % 8) + 64],
                           lh[0:W, 128 * h:128 * h + W], rh[0:W, 64 * h:64 * h + 64],
                           start=(i3 == 0), stop=(i3 == 2),
                           inc=(h % 8 == 7 and i3 == 2))
                    if h % 8 == 7:
                        bk.written()
                if ATTN_SUB <= 5:
                    return
                P.wait("vector", "dve", rinv_mark)
                for h in range(NH):
                    banks[h // 8].rwait("vector")
                    vop(lambda v, h_=h: v.tensor_scalar(
                        VT[0:W, 64 * h_:64 * h_ + 64],
                        pb[h_ // 8][0:W, 64 * (h_ % 8):64 * (h_ % 8) + 64],
                        NS[0:W, 32 + h_:33 + h_], None, ALU.mult))
                    banks[h // 8].radd("dve")
                ctm_mark = P.mark("dve")
                if ATTN_SUB <= 6:
                    return
                P.wait("tensor", "dve", ctm_mark)
                for g in range(2):
                    bk = banks[2 + g]
                    bk.acquire()
                    for cc_ in range(4):
                        c = 4 * g + cc_
                        tp(pb[2 + g][0:128, cc_ * 128:cc_ * 128 + W],
                           VT[0:W, 128 * c:128 * (c + 1)], idw, inc=(cc_ == 3))
                    bk.written()
                    bk.rwait("vector")
                    dslh = CTXFh[:].rearrange("p (c f) -> p c f", c=CH)[
                        :, 4 * g:4 * g + 4, ctx0:ctx0 + W]
                    dsll = CTXFl[:].rearrange("p (c f) -> p c f", c=CH)[
                        :, 4 * g:4 * g + 4, ctx0:ctx0 + W]
                    ssl = pb[2 + g][:].rearrange("p (c f) -> p c f", c=4)[:, :, 0:W]
                    sp_evac(ssl, dslh, dsll)
                    bk.radd("dve")

            P.wait("vector", "pe", st["ctxf_free"])
            P.wait("vector", "dy0", st["ytm_inflight"][0])
            P.wait("vector", "dy1", st["ytm_inflight"][1])
            if t == 0:
                if ATTN_PARTS is None or "mini" in ATTN_PARTS:
                    attn_block(512, 0, 8)
            else:
                if ATTN_PARTS is None or "tail" in ATTN_PARTS:
                    P.wait("vector", "dve", st["ctail_mark"])
                    for Td, Ts in ((CTXFh, CTAILh), (CTXFl, CTAILl)):
                        dsl = Td[:].rearrange("p (c f) -> p c f", c=CH)[:, :, 0:8]
                        ssl = Ts[:].rearrange("p (c f) -> p c f", c=CH)
                        vop(lambda v, d_=dsl, s_=ssl: v.tensor_copy(d_, s_))
            for b in range(NB):
                if ATTN_PARTS is None or ("b%d" % b) in ATTN_PARTS:
                    attn_block(128 * b, 8 + 128 * b, 128)
            for Td, Ts in ((CTAILh, CTXFh), (CTAILl, CTXFl)):
                dsl = Td[:].rearrange("p (c f) -> p c f", c=CH)
                ssl = Ts[:].rearrange("p (c f) -> p c f", c=CH)[:, :, 512:520]
                vop(lambda v, d_=dsl, s_=ssl: v.tensor_copy(d_, s_))
            st["ctail_mark"] = P.mark("dve")
            ctx_mark = P.mark("dve")
            tap("ctx", CTXFh[:], "dve", ctx_mark)
            if UPTO <= 3:
                return

            P.wait("tensor", "dve", ctx_mark)
            for g in range(4):
                slot = ws.consume()
                for ml in range(2):
                    m = 2 * g + ml
                    bk = banks[m % 4]
                    bk.acquire()
                    mm3(pb[m % 4][:],
                        lambda hl, k, s_=slot, ml_=ml: ws.block(s_, hl, ml_, k, 8),
                        lambda k: CTXFh[:, CT_C * k + 4:CT_C * k + 516],
                        lambda k: CTXFl[:, CT_C * k + 4:CT_C * k + 516], CH)
                    bk.written()
                    bk.rwait("vector")
                    sc_h = SC[:, 512 * (m % 2):512 * (m % 2) + 512]
                    vop(lambda v, d=sc_h, p=pb[m % 4], c=m:
                        v.tensor_scalar(d, p[:], BP[:, c:c + 1], None, ALU.add))
                    bk.radd("dve")
                    P.wait("vector", "dve", P.mark("dve"))
                    vop(lambda v, d=VF[:, QK_C * m:QK_C * m + 512], s_=sc_h, c=m:
                        v.tensor_tensor(d, s_, XRF[:, XR_C * c:XR_C * c + 512], ALU.add))
                ws.done(slot)
            rf_mark = P.mark("dve")
            tap("rf", VF[:], "dve", rf_mark)
            if UPTO <= 4:
                return

            h_dst = S1B[:].rearrange("p (c f) -> p c f", c=CH)[:, :, 0:2]
            if t == 0:
                vop(lambda v, d=h_dst: v.memset(d, 0.0))
            else:
                h_src = S1B[:].rearrange("p (c f) -> p c f", c=CH)[:, :, 512:514]
                vop(lambda v, d=h_dst, s_=h_src: v.tensor_copy(d, s_))
            halo_mark = P.mark("dve")

            def emit_norm(src_stride, src_t, dst_write, eps, subtract_mean,
                          read_guard_sem=None, read_guard=0):
                if read_guard_sem is not None:
                    P.wait("scalar", read_guard_sem, read_guard)
                    P.wait("tensor", read_guard_sem, read_guard)
                P.wait("scalar", "pe", P.mark("pe"))
                P.wait("scalar", "dve", P.mark("dve"))
                bkA, bkB = banks[4], banks[5]
                bkA.acquire()
                bkB.acquire()
                if subtract_mean:
                    for k in range(CH):
                        mm(pb[4][0:1, :], OC[:], src_t[:, src_stride * k:src_stride * k + 512],
                           start=(k == 0), stop=(k == CH - 1), inc=(k == CH - 1))
                bkA.written()
                half_guard = [0, 0]
                for c in range(CH):
                    if half_guard[c % 2]:
                        P.wait("scalar", "pe", half_guard[c % 2])
                    aop(lambda sc, c_=c: sc.activation(
                        SC[:, 512 * (c_ % 2):512 * (c_ % 2) + 512],
                        src_t[:, src_stride * c_:src_stride * c_ + 512], AF.Square))
                    P.wait("tensor", "act", P.mark("act"))
                    mm(pb[5][0:1, :], OC[:], SC[:, 512 * (c % 2):512 * (c % 2) + 512],
                       start=(c == 0), stop=(c == CH - 1), inc=True)
                    half_guard[c % 2] = P.mark("pe")
                bkB.written()
                seg = lambda i: STATS[0:1, 512 * i:512 * (i + 1)]
                bkB.rwait("vector")
                if subtract_mean:
                    bkA.rwait("vector")
                    vop(lambda v: v.tensor_scalar(seg(0), pb[4][0:1, :], 1.0 / D, None, ALU.mult))
                    bkA.radd("dve")
                vop(lambda v: v.tensor_scalar(seg(1), pb[5][0:1, :], 1.0 / D, None, ALU.mult))
                bkB.radd("dve")
                P.wait("vector", "dve", P.mark("dve"))
                if subtract_mean:
                    vop(lambda v: v.tensor_tensor(seg(2), seg(0), seg(0), ALU.mult))
                    P.wait("vector", "dve", P.mark("dve"))
                    vop(lambda v: v.tensor_tensor(seg(1), seg(1), seg(2), ALU.subtract))
                    P.wait("vector", "dve", P.mark("dve"))
                vop(lambda v, e=eps: v.tensor_scalar(seg(1), seg(1), 1.0, e, ALU.mult, ALU.add))
                vm = P.mark("dve")
                P.wait("scalar", "dve", vm)
                aop(lambda sc: sc.activation(seg(2), seg(1), AF.Ln))
                P.wait("scalar", "act", P.mark("act"))
                aop(lambda sc: sc.activation(seg(2), seg(2), AF.Exp, scale=-0.5))
                rim = P.mark("act")
                if subtract_mean:
                    P.wait("vector", "act", rim)
                    vop(lambda v: v.tensor_tensor(seg(3), seg(0), seg(2), ALU.mult))
                    P.wait("vector", "dve", P.mark("dve"))
                    vop(lambda v: v.tensor_scalar(seg(3), seg(3), -1.0, None, ALU.mult))
                nmr_mark = P.mark("dve")
                bkC, bkD = banks[6], banks[7]
                bkC.acquire()
                P.wait("tensor", "act", rim)
                mm(pb[6][:], OR[:], seg(2), start=True, stop=True, inc=True)
                bkC.written()
                if subtract_mean:
                    bkD.acquire()
                    P.wait("tensor", "dve", nmr_mark)
                    mm(pb[7][:], OR[:], seg(3), start=True, stop=True, inc=True)
                    bkD.written()
                bkC.rwait("vector")
                if subtract_mean:
                    bkD.rwait("vector")
                dst_write(bkC, bkD)
                bkC.radd("dve")
                if subtract_mean:
                    bkD.radd("dve")

            def ln1_write(bkC, bkD):
                P.wait("vector", "dve", halo_mark)
                for c in range(CH):
                    vop(lambda v, c_=c: v.tensor_tensor(
                        S1B[:, S1_C * c_ + 2:S1_C * c_ + 514],
                        VF[:, QK_C * c_:QK_C * c_ + 512], pb[6][:], ALU.mult))
                P.wait("vector", "dve", P.mark("dve"))
                for c in range(CH):
                    vop(lambda v, c_=c: v.tensor_tensor(
                        S1B[:, S1_C * c_ + 2:S1_C * c_ + 514],
                        S1B[:, S1_C * c_ + 2:S1_C * c_ + 514], pb[7][:], ALU.add))

            P.wait("vector", "dve", rf_mark)
            emit_norm(QK_C, VF, ln1_write, EPS_LN, True,
                      read_guard_sem="dve", read_guard=rf_mark)
            s1_mark = P.mark("dve")
            tap("s1", S1B[:], "dve", s1_mark)
            if UPTO <= 5:
                return

            P.wait("vector", "pe", P.mark("pe"))
            P.wait("vector", "dve", s1_mark)
            for c in range(CH):
                vop(lambda v, c_=c: v.tensor_scalar(
                    VF[:, QK_C * c_:QK_C * c_ + 512],
                    S1B[:, S1_C * c_ + 2:S1_C * c_ + 514],
                    DWK[:, 3 * c_ + 2:3 * c_ + 3], DWB[:, c_:c_ + 1], ALU.mult, ALU.add))
            for c in range(CH):
                vop(lambda v, c_=c: v.tensor_scalar(
                    P0[:, 512 * (c_ % 2):512 * (c_ % 2) + 512],
                    S1B[:, S1_C * c_ + 1:S1_C * c_ + 513],
                    DWK[:, 3 * c_ + 1:3 * c_ + 2], None, ALU.mult))
                P.wait("vector", "dve", P.mark("dve"))
                vop(lambda v, c_=c: v.tensor_tensor(
                    VF[:, QK_C * c_:QK_C * c_ + 512], VF[:, QK_C * c_:QK_C * c_ + 512],
                    P0[:, 512 * (c_ % 2):512 * (c_ % 2) + 512], ALU.add))
            for c in range(CH):
                vop(lambda v, c_=c: v.tensor_scalar(
                    P0[:, 512 * (c_ % 2):512 * (c_ % 2) + 512],
                    S1B[:, S1_C * c_:S1_C * c_ + 512],
                    DWK[:, 3 * c_:3 * c_ + 1], None, ALU.mult))
                P.wait("vector", "dve", P.mark("dve"))
                vop(lambda v, c_=c: v.tensor_tensor(
                    VF[:, QK_C * c_:QK_C * c_ + 512], VF[:, QK_C * c_:QK_C * c_ + 512],
                    P0[:, 512 * (c_ % 2):512 * (c_ % 2) + 512], ALU.add))
            for c in range(CH):
                sp_evac(VF[:, QK_C * c:QK_C * c + 512],
                        QFh[:, QK_C * c:QK_C * c + 512],
                        QFl[:, QK_C * c:QK_C * c + 512])
            cv_mark = P.mark("dve")
            tap("cv", VF[:], "dve", cv_mark)
            if UPTO <= 6:
                return

            P.wait("tensor", "dve", cv_mark)
            for g in range(4):
                slot = ws.consume()
                for ml in range(2):
                    m = 2 * g + ml
                    bk = banks[m % 4]
                    bk.acquire()
                    mm3(pb[m % 4][:],
                        lambda hl, k, s_=slot, ml_=ml: ws.block(s_, hl, ml_, k, 8),
                        lambda k: QFh[:, QK_C * k:QK_C * k + 512],
                        lambda k: QFl[:, QK_C * k:QK_C * k + 512], CH)
                    bk.written()
                    bk.rwait("scalar")
                    sc_h = SC[:, 512 * (m % 2):512 * (m % 2) + 512]
                    aop(lambda sc, d=sc_h, p=pb[m % 4], c=m:
                        sc.activation(d, p[:], AF.Silu, bias=PWB[:, c:c + 1]))
                    bk.radd("act")
                    P.wait("vector", "act", P.mark("act"))
                    vop(lambda v, c_=m, s_=sc_h: v.tensor_tensor(
                        XRF[:, XR_C * c_:XR_C * c_ + 512],
                        S1B[:, S1_C * c_ + 2:S1_C * c_ + 514], s_, ALU.add))
                ws.done(slot)
            P.wait("vector", "dve", P.mark("dve"))
            for c in range(CH):
                sp_evac(XRF[:, XR_C * c:XR_C * c + 512],
                        XRFh[:, XR_C * c:XR_C * c + 512],
                        XRFl[:, XR_C * c:XR_C * c + 512])
            x1_mark = P.mark("dve")
            tap("x1", XRF[:], "dve", x1_mark)
            if UPTO <= 7:
                return

            P.wait("tensor", "dve", x1_mark)
            for mt in range(4):
                bk = banks[4 + mt]
                bk.acquire()
                for k in range(CH):
                    mm(pb[4 + mt][0:128, 0:8],
                       XRF[:, XR_C * k + 128 * mt:XR_C * k + 128 * (mt + 1)],
                       RT[:, 8 * k:8 * k + 8],
                       start=(k == 0), stop=(k == CH - 1), inc=(k == CH - 1))
                bk.written()
                bk.rwait("vector")
                vop(lambda v, m_=mt: v.tensor_tensor(
                    LG[:, 8 * m_:8 * m_ + 8], pb[4 + m_][0:128, 0:8], RBB[:], ALU.add))
                bk.radd("dve")
            col = lambda t_, i: t_[:, i:i + 1]
            for mt in range(4):
                lg = LG[:, 8 * mt:8 * mt + 8]
                m0c, m1c = col(SM, mt), col(SM, 4 + mt)
                w0c, w1c = col(SM, 8 + mt), col(SM, 12 + mt)
                ec, e1c = col(SM, 16 + mt), col(SM, 20 + mt)
                eq0 = SM[:, 24 + 8 * mt:32 + 8 * mt]
                eq1 = RWV[:, 8 * mt:8 * mt + 8]

                def step(f):
                    vop(f)
                    P.wait("vector", "dve", P.mark("dve"))
                step(lambda v, a=m0c, b=lg: v.tensor_reduce(a, b, axis=AX, op=ALU.max))
                step(lambda v, a=eq0, b=lg, c=m0c: v.tensor_scalar(a, b, c, None, ALU.is_ge))
                step(lambda v, a=eq1, b=eq0: v.tensor_scalar(a, b, -1e30, None, ALU.mult))
                step(lambda v, a=eq1, b=lg: v.tensor_tensor(a, a, b, ALU.add))
                step(lambda v, a=m1c, b=eq1: v.tensor_reduce(a, b, axis=AX, op=ALU.max))
                step(lambda v, a=eq1, b=lg, c=m1c: v.tensor_scalar(a, b, c, None, ALU.is_ge))
                step(lambda v, a=eq1, b=eq0: v.tensor_tensor(a, a, b, ALU.subtract))
                step(lambda v, a=ec, b=m1c, c=m0c: v.tensor_tensor(a, b, c, ALU.subtract))
                P.wait("scalar", "dve", P.mark("dve"))
                aop(lambda sc, a=ec: sc.activation(a, a, AF.Exp))
                P.wait("vector", "act", P.mark("act"))
                step(lambda v, a=e1c, b=ec: v.tensor_scalar(a, b, 1.0, 1.0, ALU.mult, ALU.add))
                step(lambda v, a=w1c, b=e1c: v.reciprocal(a, b))
                step(lambda v, a=w1c, b=ec: v.tensor_tensor(a, a, b, ALU.mult))
                step(lambda v, a=w0c, b=w1c: v.tensor_scalar(a, b, -1.0, 1.0, ALU.mult, ALU.add))
                step(lambda v, a=eq0, b=w0c: v.tensor_scalar(a, a, b, None, ALU.mult))
                step(lambda v, a=eq1, b=w1c: v.tensor_scalar(a, a, b, None, ALU.mult))
                step(lambda v, a=eq1, b=eq0: v.tensor_tensor(a, a, b, ALU.add))
            rw_mark = P.mark("dve")
            tap("lg", LG[:], "dve", rw_mark)
            tap("rw", RWV[:], "dve", rw_mark)
            P.wait("tensor", "dve", rw_mark)
            bk = banks[4]
            bk.acquire()
            for mt in range(4):
                tp(pb[4][0:8, 128 * mt:128 * (mt + 1)], RWV[:, 8 * mt:8 * mt + 8],
                   IDN[:, :], inc=(mt == 3))
            bk.written()
            bk.rwait("vector")
            vop(lambda v: v.tensor_copy(RWF8[0:8, :], pb[4][0:8, 0:512]))
            bk.radd("dve")
            rwf_mark = P.mark("dve")
            if UPTO <= 8:
                return

            P.wait("tensor", "dve", rwf_mark)
            for e in range(E):
                rwbc_bank = 6 + e % 2
                nb_ = banks[rwbc_bank]
                nb_.acquire()
                mm(pb[rwbc_bank][:], SEL[0:8, 128 * e:128 * (e + 1)], RWF8[0:8, :],
                   start=True, stop=True, inc=True)
                nb_.written()
                for m in range(CH):
                    gi = e * 8 + m
                    if gi % 2 == 0:
                        slot = ws.consume()
                    bk = banks[gi % 4]
                    bk.acquire()
                    mm3(pb[gi % 4][:],
                        lambda hl, k, s_=slot, ml_=gi % 2: ws.block(s_, hl, ml_, k, 8),
                        lambda k: XRFh[:, XR_C * k:XR_C * k + 512],
                        lambda k: XRFl[:, XR_C * k:XR_C * k + 512], CH)
                    bk.written()
                    bk.rwait("scalar")
                    sc_h = SC[:, 512 * (m % 2):512 * (m % 2) + 512]
                    aop(lambda sc, d=sc_h, p=pb[gi % 4], e_=e, m_=m:
                        sc.activation(d, p[:], AF.Silu, bias=EB[:, 8 * e_ + m_:8 * e_ + m_ + 1]))
                    bk.radd("act")
                    am = P.mark("act")
                    P.wait("vector", "act", am)
                    banks[rwbc_bank].rwait("vector")
                    if e == 0:
                        vop(lambda v, d=COMB[:, 512 * m:512 * m + 512], s_=sc_h,
                            rb=pb[rwbc_bank]: v.tensor_tensor(d, s_, rb[:], ALU.mult))
                    else:
                        vop(lambda v, s_=sc_h, rb=pb[rwbc_bank], m_=m:
                            v.tensor_tensor(P0[:, 512 * (m_ % 2):512 * (m_ % 2) + 512],
                                            s_, rb[:], ALU.mult))
                        P.wait("vector", "dve", P.mark("dve"))
                        vop(lambda v, d=COMB[:, 512 * m:512 * m + 512], m_=m:
                            v.tensor_tensor(d, d, P0[:, 512 * (m_ % 2):512 * (m_ % 2) + 512],
                                            ALU.add))
                    banks[rwbc_bank].radd("dve")
                    if gi % 2 == 1:
                        ws.done(slot)
            P.wait("vector", "pe", P.mark("pe"))
            P.wait("vector", "dve", P.mark("dve"))
            for c in range(CH):
                sp_evac(COMB[:, 512 * c:512 * c + 512],
                        KFh[:, QK_C * c:QK_C * c + 512],
                        KFl[:, QK_C * c:QK_C * c + 512])
            comb_mark = P.mark("dve")
            tap("comb", COMB[:], "dve", comb_mark)
            if UPTO <= 9:
                return

            P.wait("tensor", "dve", comb_mark)
            for g in range(4):
                slot = ws.consume()
                for ml in range(2):
                    m = 2 * g + ml
                    bk = banks[m % 4]
                    bk.acquire()
                    mm3(pb[m % 4][:],
                        lambda hl, k, s_=slot, ml_=ml: ws.block(s_, hl, ml_, k, 8),
                        lambda k: KFh[:, QK_C * k:QK_C * k + 512],
                        lambda k: KFl[:, QK_C * k:QK_C * k + 512], CH)
                    bk.written()
                    bk.rwait("vector")
                    sc_h = SC[:, 512 * (m % 2):512 * (m % 2) + 512]
                    vop(lambda v, d=sc_h, p=pb[m % 4], c=m:
                        v.tensor_scalar(d, p[:], OB[:, c:c + 1], None, ALU.add))
                    bk.radd("dve")
                    P.wait("vector", "dve", P.mark("dve"))
                    vop(lambda v, c=m, s_=sc_h: v.tensor_tensor(
                        XRF[:, XR_C * c:XR_C * c + 512],
                        XRF[:, XR_C * c:XR_C * c + 512], s_, ALU.add))
                ws.done(slot)
            hp_mark = P.mark("dve")
            tap("hp", XRF[:], "dve", hp_mark)
            if UPTO <= 10:
                return

            def rms_write(bkC, bkD):
                for c in range(CH):
                    vop(lambda v, c_=c: v.tensor_tensor(
                        VF[:, QK_C * c_:QK_C * c_ + 512],
                        XRF[:, XR_C * c_:XR_C * c_ + 512], pb[6][:], ALU.mult))
                P.wait("vector", "dve", P.mark("dve"))
                for c in range(CH):
                    vop(lambda v, c_=c: v.tensor_scalar(
                        VF[:, QK_C * c_:QK_C * c_ + 512],
                        VF[:, QK_C * c_:QK_C * c_ + 512],
                        NW[:, c_:c_ + 1], None, ALU.mult))

            P.wait("vector", "dve", hp_mark)
            P.wait("vector", "pe", P.mark("pe"))
            emit_norm(XR_C, XRF, rms_write, EPS_RMS, False,
                      read_guard_sem="dve", read_guard=hp_mark)
            P.wait("vector", "dve", P.mark("dve"))
            for c in range(CH):
                sp_evac(VF[:, QK_C * c:QK_C * c + 512],
                        CTXFh[:, CT_C * c:CT_C * c + 512],
                        CTXFl[:, CT_C * c:CT_C * c + 512])
            h_mark = P.mark("dve")
            tap("h", VF[:], "dve", h_mark)
            if UPTO <= 11:
                return

            P.wait("tensor", "dve", h_mark)
            slot = ws.consume()
            for m in range(2):
                bk = banks[m]
                bk.acquire()
                mm3(pb[m][:],
                    lambda hl, k, s_=slot, m_=m: ws.block(s_, hl, m_, k, 8),
                    lambda k: CTXFh[:, CT_C * k:CT_C * k + 512],
                    lambda k: CTXFl[:, CT_C * k:CT_C * k + 512], CH)
                bk.written()
                bk.rwait("scalar")
                aop(lambda sc, d=FF32[:, 512 * m:512 * m + 512], p=pb[m], c=m:
                    sc.activation(d, p[:], AF.Gelu, bias=FDB[:, c:c + 1]))
                bk.radd("act")
            ws.done(slot)
            st["ctxf_free"] = P.mark("pe")
            ff1_mark = P.mark("act")
            P.wait("vector", "act", ff1_mark)
            P.wait("vector", "pe", P.mark("pe"))
            for m in range(2):
                sp_evac(FF32[:, 512 * m:512 * m + 512],
                        VTh[:, 512 * m:512 * m + 512],
                        VTl[:, 512 * m:512 * m + 512])
            ff1_mark = P.mark("dve")
            tap("ff1", FF32[:], "dve", ff1_mark)
            if UPTO <= 12:
                return

            P.wait("tensor", "dve", ff1_mark)
            slot = ws.consume()
            for m in range(CH):
                bk = banks[m % 4]
                bk.acquire()
                mm3(pb[m % 4][:],
                    lambda hl, k, s_=slot, m_=m: ws.block(s_, hl, m_, k, 2),
                    lambda k: VTh[:, 512 * k:512 * k + 512],
                    lambda k: VTl[:, 512 * k:512 * k + 512], 2)
                bk.written()
                bk.rwait("vector")
                sc_h = SC[:, 512 * (m % 2):512 * (m % 2) + 512]
                vop(lambda v, d=sc_h, p=pb[m % 4], c=m:
                    v.tensor_scalar(d, p[:], FUB[:, c:c + 1], None, ALU.add))
                bk.radd("dve")
                P.wait("vector", "dve", P.mark("dve"))
                vop(lambda v, c=m, s_=sc_h: v.tensor_tensor(
                    XRF[:, XR_C * c:XR_C * c + 512], s_,
                    VF[:, QK_C * c:QK_C * c + 512], ALU.add))
                P.wait("vector", "dve", P.mark("dve"))
                vop(lambda v, c=m: v.tensor_tensor(
                    XRF[:, XR_C * c:XR_C * c + 512],
                    XRF[:, XR_C * c:XR_C * c + 512],
                    S1B[:, S1_C * c + 2:S1_C * c + 514], ALU.add))
            ws.done(slot)
            yp_mark = P.mark("dve")
            tap("yp", XRF[:], "dve", yp_mark)
            if UPTO <= 13:
                return

            def ln3_write(bkC, bkD):
                for c in range(CH):
                    vop(lambda v, c_=c: v.tensor_tensor(
                        VF[:, QK_C * c_:QK_C * c_ + 512],
                        XRF[:, XR_C * c_:XR_C * c_ + 512], pb[6][:], ALU.mult))
                P.wait("vector", "dve", P.mark("dve"))
                for c in range(CH):
                    vop(lambda v, c_=c: v.tensor_tensor(
                        VF[:, QK_C * c_:QK_C * c_ + 512],
                        VF[:, QK_C * c_:QK_C * c_ + 512], pb[7][:], ALU.add))

            P.wait("vector", "dve", yp_mark)
            emit_norm(XR_C, XRF, ln3_write, EPS_LN, True,
                      read_guard_sem="dve", read_guard=yp_mark)
            yf_mark = P.mark("dve")
            tap("yf", VF[:], "dve", yf_mark)
            if UPTO <= 14:
                return

            P.wait("tensor", "dve", yf_mark)
            for mt in range(4):
                for g in range(2):
                    bk = banks[2 * (mt % 2) + g]
                    bk.acquire()
                    for dd in range(4):
                        d = 4 * g + dd
                        tp(pb[2 * (mt % 2) + g][0:128, dd * 128:(dd + 1) * 128],
                           VF[:, QK_C * d + 128 * mt:QK_C * d + 128 * (mt + 1)],
                           IDN[:, :], inc=(dd == 3))
                    bk.written()
                    bk.rwait("vector")
                    P.wait("vector", "dy0", st["ytm_inflight"][0])
                    vop(lambda v, g_=g, b_=2 * (mt % 2) + g:
                        v.tensor_copy(VTY[:, 512 * g_:512 * (g_ + 1)], pb[b_][:]))
                    bk.radd("dve")
                em = P.mark("dve")
                P.wait("gpsimd", "dve", em)
                P.op("gpsimd",
                     lambda g_, s_=s, o=base + 128 * mt:
                     g_.dma_start(out=y[s_, o:o + 128, :], in_=VTY[:]),
                     inc="dy0", n=16)
                st["ytm_inflight"][0] = P.mark("dy0")

        ntiles = SEQS * NT if LIMIT_TILES is None else LIMIT_TILES
        done_tiles = 0
        for s in range(SEQS):
            for t in range(NT):
                if done_tiles < ntiles:
                    emit_tile(s, t)
                    done_tiles += 1

        P.wait("gpsimd", "dy0", P.mark("dy0"))
        P.wait("gpsimd", "dy1", P.mark("dy1"))
        if taps:
            P.wait("gpsimd", "ddbg", P.mark("ddbg"))

        P.check_deadlock()
        with nc.Block() as block:
            P.replay(block)

    return nc


def _prep(inputs):
    g = {k: np.asarray(v) for k, v in inputs.items()}

    def blk(w, nm, nk):
        return np.ascontiguousarray(
            w.reshape(nk, 128, nm, 128).transpose(2, 0, 1, 3).astype(np.float32))

    out = {}
    out["wqkv_b"] = blk(g["L0_attn_w_qkv"], 24, 8)
    out["wproj_b"] = blk(g["L0_attn_w_proj"], 8, 8)
    out["wpw_b"] = blk(np.ascontiguousarray(g["L0_ffn_pw_w"].T), 8, 8)
    out["wexp_b"] = np.ascontiguousarray(
        np.stack([blk(np.ascontiguousarray(g["L0_ffn_expert_w"][e].T), 8, 8)
                  for e in range(8)]).reshape(64, 8, 128, 128))
    out["wout_b"] = blk(np.ascontiguousarray(g["L0_ffn_out_w"].T), 8, 8)
    out["wffd_b"] = blk(np.ascontiguousarray(g["L0_ffn_ff_down_w"].T), 2, 8)
    out["wffu_b"] = blk(np.ascontiguousarray(g["L0_ffn_ff_up_w"].T), 8, 2)

    def pmaj(a, *shape):
        a = np.asarray(a, np.float32).reshape(*shape)
        return np.ascontiguousarray(np.moveaxis(a, -1, 0).reshape(128, -1))

    bt = g["L0_attn_bias_table"].astype(np.float32)
    idx = 112 + (np.arange(8)[:, None] - np.arange(8)[None, :])
    bias8 = bt[idx.reshape(-1), :].reshape(8, 8, NH).transpose(2, 0, 1)
    M = np.full((NH, 128, 128), -1e9, np.float32)
    for w in range(16):
        M[:, 8 * w:8 * w + 8, 8 * w:8 * w + 8] = bias8
    out["mask_c"] = np.ascontiguousarray(M.transpose(1, 0, 2).reshape(128, NH * 128))
    out["ident_c"] = np.eye(128, dtype=np.float32)
    out["onescol_c"] = np.ones((128, 1), np.float32)
    out["onesrow_c"] = np.ones((1, 128), np.float32)
    out["sel_c"] = np.ascontiguousarray(
        np.kron(np.eye(8, dtype=np.float32), np.ones((1, 128), np.float32)))
    out["bqkv_c"] = pmaj(g["L0_attn_b_qkv"], 24, 128)
    out["bproj_c"] = pmaj(g["L0_attn_b_proj"], 8, 128)
    out["pwb_c"] = pmaj(g["L0_ffn_pw_b"], 8, 128)
    out["outb_c"] = pmaj(g["L0_ffn_out_b"], 8, 128)
    out["ffub_c"] = pmaj(g["L0_ffn_ff_up_b"], 8, 128)
    out["ffdb_c"] = pmaj(g["L0_ffn_ff_down_b"], 2, 128)
    out["nw_c"] = pmaj(g["L0_ffn_norm_w"], 8, 128)
    out["dwb_c"] = pmaj(g["L0_ffn_dw_bias"], 8, 128)
    dwk = np.asarray(g["L0_ffn_dw_kernel"], np.float32).reshape(8, 128, 3)
    out["dwk_c"] = np.ascontiguousarray(dwk.transpose(1, 0, 2).reshape(128, 24))
    rt = np.asarray(g["L0_ffn_router_w"], np.float32).T.reshape(8, 128, 8)
    out["rt_c"] = np.ascontiguousarray(rt.transpose(1, 0, 2).reshape(128, 64))
    out["rbbc_c"] = np.ascontiguousarray(
        np.broadcast_to(g["L0_ffn_router_b"].astype(np.float32)[None, :], (128, 8)))
    eb = np.asarray(g["L0_ffn_expert_b"], np.float32).reshape(8, 8, 128)
    out["eb_c"] = np.ascontiguousarray(eb.transpose(2, 0, 1).reshape(128, 64))
    return out


def kernel(**inputs):
    nc = build_nc()
    shared = _prep(inputs)
    x = np.ascontiguousarray(np.asarray(inputs["L0_attn_x"], dtype=np.float32))
    in_maps = []
    for c in range(NCORES):
        m = dict(shared)
        m["xin"] = np.ascontiguousarray(x[SEQS * c:SEQS * (c + 1)])
        in_maps.append(m)
    res = run_bass_kernel_spmd(nc, in_maps, core_ids=list(range(NCORES)))
    return np.concatenate([r["y"] for r in res.results], axis=0)


# version 03
# speedup vs baseline: 1.0650x; 1.0650x over previous
import contextlib
import numpy as np
import concourse.bass as bass
import concourse.mybir as mybir
from concourse.bass_utils import run_bass_kernel_spmd

F32 = mybir.dt.float32
F16 = mybir.dt.float16
AF = mybir.ActivationFunctionType
ALU = mybir.AluOpType
AX = mybir.AxisListType.X

NCORES = 8
SEQS = 2
N = 4096
D = 1024
CH = 8
T = 512
NT = N // T
NB = T // 128
NH = 16
WS = 8
E = 8
EPS_LN = 1e-5
EPS_RMS = 1e-6

XR_C = 524
QK_C = 520
CT_C = 520
S1_C = 514

ENGINES = ("tensor", "vector", "scalar", "sync", "gpsimd")

DEBUG_TAPS = set()
TAP_S = 0
TAP_T = 0
LIMIT_TILES = None
UPTO = 99
ATTN_PARTS = None
ATTN_SUB = 99


class Prog:

    def __init__(self, nc):
        self.nc = nc
        self.ops = {e: [] for e in ENGINES}
        self.sems = {}
        self.cnt = {}
        self.trace = {e: [] for e in ENGINES}

    def add_sem(self, name, handle):
        self.sems[name] = handle
        self.cnt[name] = 0

    def op(self, eng, fn, inc=None, n=1):
        if inc is not None:
            self.cnt[inc] += n
            sem = self.sems[inc]

            def run(e, fn=fn, sem=sem, n=n):
                fn(e).then_inc(sem, n)
        else:
            def run(e, fn=fn):
                fn(e)
        self.trace[eng].append(("inc", inc, n) if inc else ("inc", None, 0))
        self.ops[eng].append(run)
        return self.cnt[inc] if inc is not None else None

    def wait(self, eng, sem, val):
        assert val <= self.cnt[sem], f"wait {sem}>={val} but only {self.cnt[sem]} emitted"
        if val <= 0:
            return
        s = self.sems[sem]
        self.trace[eng].append(("wait", sem, val))
        self.ops[eng].append(lambda e, s=s, v=val: e.wait_ge(s, v))

    def check_deadlock(self):
        pos = {e: 0 for e in ENGINES}
        cur = {s: 0 for s in self.cnt}
        progress = True
        while progress:
            progress = False
            for e in ENGINES:
                tr = self.trace[e]
                while pos[e] < len(tr):
                    kind, sem, v = tr[pos[e]]
                    if kind == "wait" and cur[sem] < v:
                        break
                    if kind == "inc" and sem is not None:
                        cur[sem] += v
                    pos[e] += 1
                    progress = True
        stuck = {e: (pos[e], len(self.trace[e])) for e in ENGINES
                 if pos[e] < len(self.trace[e])}
        if stuck:
            detail = {e: self.trace[e][pos[e]] for e in stuck}
            raise RuntimeError(f"deadlock: stuck engines {stuck} at {detail}")

    def mark(self, sem):
        return self.cnt[sem]

    def replay(self, block):
        for eng in ENGINES:
            ops = self.ops[eng]
            if not ops:
                continue

            def body(handle, ops=ops):
                for f in ops:
                    f(handle)
            getattr(block, eng)(body)


class Bank:

    def __init__(self, prog, ap):
        self.P = prog
        self.ap = ap
        self.wmark = 0
        self.readers = []

    def acquire(self):
        for sem, val in self.readers:
            self.P.wait("tensor", sem, val)
        self.readers = []

    def written(self):
        self.wmark = self.P.mark("pe")

    def rwait(self, eng):
        self.P.wait(eng, "pe", self.wmark)

    def radd(self, sem):
        self.readers.append((sem, self.P.mark(sem)))


class WStream:

    def __init__(self, prog, slots, semnames):
        self.P = prog
        self.slots = slots
        self.sems = semnames
        self.q = []
        self.issued = []
        self.guard = [0, 0]
        self.nxt = 0

    def push(self, scr_ap, g):
        self.q.append((scr_ap, g))

    def _issue_one(self):
        scr_ap, g = self.q.pop(0)
        s = self.nxt
        self.nxt ^= 1
        self.P.wait("sync", "pe", self.guard[s])
        dst = self.slots[s][:].rearrange("p (hl f) -> p hl f", hl=2)
        src = scr_ap[g].rearrange("hl p f -> p hl f")
        self.P.op("sync", lambda sy, d=dst, sr=src: sy.dma_start(out=d, in_=sr),
                  inc=self.sems[s], n=16)
        self.issued.append((s, self.P.mark(self.sems[s])))

    def prime(self):
        while len(self.issued) < 2 and self.q:
            self._issue_one()

    def consume(self):
        s, target = self.issued.pop(0)
        self.P.wait("tensor", self.sems[s], target)
        return s

    def block(self, s, hl, m_local, k, nk):
        i = 2048 * hl + (m_local * nk + k) * 128
        return self.slots[s][:, i:i + 128]

    def done(self, s):
        self.guard[s] = self.P.mark("pe")
        if self.q:
            self._issue_one()


def build_nc():
    nc = bass.Bass("TRN2", target_bir_lowering=False)

    xin = nc.dram_tensor("xin", [SEQS, N, D], F32, kind="ExternalInput").ap()
    y = nc.dram_tensor("y", [SEQS, N, D], F32, kind="ExternalOutput").ap()

    wqkv_b = nc.dram_tensor("wqkv_b", [24, 8, 128, 128], F32, kind="ExternalInput").ap()
    wproj_b = nc.dram_tensor("wproj_b", [8, 8, 128, 128], F32, kind="ExternalInput").ap()
    wpw_b = nc.dram_tensor("wpw_b", [8, 8, 128, 128], F32, kind="ExternalInput").ap()
    wexp_b = nc.dram_tensor("wexp_b", [64, 8, 128, 128], F32, kind="ExternalInput").ap()
    wout_b = nc.dram_tensor("wout_b", [8, 8, 128, 128], F32, kind="ExternalInput").ap()
    wffd_b = nc.dram_tensor("wffd_b", [2, 8, 128, 128], F32, kind="ExternalInput").ap()
    wffu_b = nc.dram_tensor("wffu_b", [8, 2, 128, 128], F32, kind="ExternalInput").ap()

    KINDS = [
        ("qkv", wqkv_b, 12, 2, 8),
        ("proj", wproj_b, 4, 2, 8),
        ("pw", wpw_b, 4, 2, 8),
        ("exp", wexp_b, 32, 2, 8),
        ("out", wout_b, 4, 2, 8),
        ("ffd", wffd_b, 1, 2, 8),
        ("ffu", wffu_b, 1, 8, 2),
    ]
    scr = {}
    for nm, _, ng, _, _ in KINDS:
        scr[nm] = nc.dram_tensor("scr_" + nm, [ng, 2, 128, 2048], F16).ap()

    mask_c = nc.dram_tensor("mask_c", [128, NH * 128], F32, kind="ExternalInput").ap()
    ident_c = nc.dram_tensor("ident_c", [128, 128], F32, kind="ExternalInput").ap()
    onescol_c = nc.dram_tensor("onescol_c", [128, 1], F32, kind="ExternalInput").ap()
    onesrow_c = nc.dram_tensor("onesrow_c", [1, 128], F32, kind="ExternalInput").ap()
    sel_c = nc.dram_tensor("sel_c", [8, 8 * 128], F32, kind="ExternalInput").ap()
    bqkv_c = nc.dram_tensor("bqkv_c", [128, 24], F32, kind="ExternalInput").ap()
    bproj_c = nc.dram_tensor("bproj_c", [128, 8], F32, kind="ExternalInput").ap()
    pwb_c = nc.dram_tensor("pwb_c", [128, 8], F32, kind="ExternalInput").ap()
    outb_c = nc.dram_tensor("outb_c", [128, 8], F32, kind="ExternalInput").ap()
    ffub_c = nc.dram_tensor("ffub_c", [128, 8], F32, kind="ExternalInput").ap()
    ffdb_c = nc.dram_tensor("ffdb_c", [128, 2], F32, kind="ExternalInput").ap()
    nw_c = nc.dram_tensor("nw_c", [128, 8], F32, kind="ExternalInput").ap()
    dwb_c = nc.dram_tensor("dwb_c", [128, 8], F32, kind="ExternalInput").ap()
    dwk_c = nc.dram_tensor("dwk_c", [128, 24], F32, kind="ExternalInput").ap()
    rt_c = nc.dram_tensor("rt_c", [128, 64], F32, kind="ExternalInput").ap()
    rbbc_c = nc.dram_tensor("rbbc_c", [128, 8], F32, kind="ExternalInput").ap()
    eb_c = nc.dram_tensor("eb_c", [128, 64], F32, kind="ExternalInput").ap()

    taps = {}
    for tag, w in [("xrf", CH * XR_C), ("qf", CH * QK_C), ("kf", CH * QK_C),
                   ("vf", CH * QK_C), ("ctx", CH * CT_C), ("rf", CH * QK_C),
                   ("s1", CH * S1_C), ("cv", CH * QK_C), ("x1", CH * XR_C),
                   ("lg", 32), ("rw", 32), ("comb", CH * 512), ("hp", CH * XR_C),
                   ("h", CH * QK_C), ("ff1", 2 * 512), ("yp", CH * XR_C),
                   ("yf", CH * QK_C)]:
        if tag in DEBUG_TAPS:
            taps[tag] = nc.dram_tensor("dbg_" + tag, [128, w], F32,
                                       kind="ExternalOutput").ap()

    with contextlib.ExitStack() as ctx:
        def sbuf(name, shape):
            return ctx.enter_context(nc.sbuf_tensor(name, shape, F32))

        def sbuf16(name, shape):
            return ctx.enter_context(nc.sbuf_tensor(name, shape, F16))

        XS = sbuf("XS", [128, 4 * 1024])
        XRF = sbuf("XRF", [128, CH * XR_C])
        VF = sbuf("VF", [128, CH * QK_C])
        COMB = sbuf("COMB", [128, CH * 512])
        S1B = sbuf("S1B", [128, CH * S1_C])
        P0 = sbuf("P0", [128, NH * 128])
        VTY = P0[:, 0:1024]
        VT = sbuf("VT", [128, 1024])
        SC = sbuf("SC", [128, 2 * 512])
        FF32 = SC
        XRFh = sbuf16("XRFh", [128, CH * XR_C])
        XRFl = sbuf16("XRFl", [128, CH * XR_C])
        QFh = sbuf16("QFh", [128, CH * QK_C])
        QFl = sbuf16("QFl", [128, CH * QK_C])
        KFh = sbuf16("KFh", [128, CH * QK_C])
        KFl = sbuf16("KFl", [128, CH * QK_C])
        CTXFh = sbuf16("CTXFh", [128, CH * CT_C])
        CTXFl = sbuf16("CTXFl", [128, CH * CT_C])
        PTSh = XRFh[:, 0:NH * 128]
        PTSl = XRFl[:, 0:NH * 128]
        VTh = sbuf16("VTh", [128, 1024])
        VTl = sbuf16("VTl", [128, 1024])
        CTAILh = sbuf16("CTAILh", [128, CH * 8])
        CTAILl = sbuf16("CTAILl", [128, CH * 8])
        LG = sbuf("LG", [128, 32])
        SM = sbuf("SM", [128, 64])
        RWV = sbuf("RWV", [128, 32])
        RWF8 = sbuf("RWF8", [8, 512])
        SEL = sbuf("SEL", [8, 8 * 128])
        STATS = sbuf("STATS", [1, 4 * 512])
        NS = sbuf("NS", [128, 48])
        MSK = sbuf("MSK", [128, NH * 128])
        IDN = sbuf("IDN", [128, 128])
        OC = sbuf("OC", [128, 1])
        OR = sbuf("OR", [1, 128])
        BQ = sbuf("BQ", [128, 24])
        BP = sbuf("BP", [128, 8])
        PWB = sbuf("PWB", [128, 8])
        OB = sbuf("OB", [128, 8])
        FUB = sbuf("FUB", [128, 8])
        FDB = sbuf("FDB", [128, 2])
        NW = sbuf("NW", [128, 8])
        DWB = sbuf("DWB", [128, 8])
        DWK = sbuf("DWK", [128, 24])
        RT = sbuf("RT", [128, 64])
        RBB = sbuf("RBB", [128, 8])
        EB = sbuf("EB", [128, 64])
        WA = sbuf16("WA", [128, 4096])
        WB = sbuf16("WB", [128, 4096])
        W32A = COMB[:, 0:2048]
        W32B = P0[:, 0:2048]

        pb = [ctx.enter_context(nc.psum_tensor(f"pb{i}", [128, 512], F32))
              for i in range(8)]

        P = Prog(nc)
        for sname in ("pe", "dve", "act", "gp", "dwa", "dwb", "dx", "dy0",
                      "dy1", "dc", "ddbg", "s32a", "s32b", "dscr"):
            P.add_sem(sname, ctx.enter_context(nc.semaphore(sname)))
        banks = [Bank(P, t) for t in pb]
        ws = WStream(P, [WA, WB], ["dwa", "dwb"])

        def vop(fn):
            return P.op("vector", fn, inc="dve")

        def aop(fn):
            return P.op("scalar", fn, inc="act")

        def mm(out_ap, lhs_ap, rhs_ap, start, stop, inc=False, tile_position=None):
            def f(t, o=out_ap, l=lhs_ap, r=rhs_ap, st=start, sp=stop, tp=tile_position):
                if tp is not None:
                    return t.matmul(o, l, r, start=st, stop=sp, tile_position=tp)
                return t.matmul(o, l, r, start=st, stop=sp)
            return P.op("tensor", f, inc="pe" if inc else None)

        def tp(out_ap, in_ap, id_ap, inc=False):
            return P.op("tensor",
                        lambda t, o=out_ap, i=in_ap, d=id_ap: t.transpose(o, i, d),
                        inc="pe" if inc else None)

        def mm3(bank_ap, blkfn, rh, rl, nk, tpos=None):
            i, n = 0, 3 * nk
            for hl, rf in ((0, rh), (1, rh), (0, rl)):
                for k in range(nk):
                    mm(bank_ap, blkfn(hl, k), rf(k),
                       start=(i == 0), stop=(i == n - 1), inc=(i == n - 1),
                       tile_position=tpos)
                    i += 1

        def sp_evac(src32, dh, dl):
            vop(lambda v, s_=src32, d_=dh: v.tensor_copy(d_, s_))
            P.wait("vector", "dve", P.mark("dve"))
            vop(lambda v, s_=src32, h_=dh, d_=dl:
                v.tensor_tensor(d_, s_, h_, ALU.subtract))

        def sp_wave(triples):
            for src32, dh, dl in triples:
                vop(lambda v, s_=src32, d_=dh: v.tensor_copy(d_, s_))
            P.wait("vector", "dve", P.mark("dve"))
            for src32, dh, dl in triples:
                vop(lambda v, s_=src32, h_=dh, d_=dl:
                    v.tensor_tensor(d_, s_, h_, ALU.subtract))

        def tap(tag, src_ap, prod_sem, prod_val):
            if tag not in taps:
                return
            P.wait("gpsimd", prod_sem, prod_val)
            P.op("gpsimd", lambda g, o=taps[tag], s=src_ap: g.dma_start(out=o, in_=s),
                 inc="ddbg", n=16)

        def cload(dst, src):
            P.op("sync", lambda sy, d=dst, s=src: sy.dma_start(out=d, in_=s),
                 inc="dc", n=16)

        cload(MSK[:], mask_c)
        cload(IDN[:], ident_c)
        cload(OC[:], onescol_c)
        cload(OR[:], onesrow_c)
        cload(SEL[:], sel_c)
        cload(BQ[:], bqkv_c)
        cload(BP[:], bproj_c)
        cload(PWB[:], pwb_c)
        cload(OB[:], outb_c)
        cload(FUB[:], ffub_c)
        cload(FDB[:], ffdb_c)
        cload(NW[:], nw_c)
        cload(DWB[:], dwb_c)
        cload(DWK[:], dwk_c)
        cload(RT[:], rt_c)
        cload(RBB[:], rbbc_c)
        cload(EB[:], eb_c)
        dc_total = P.mark("dc")
        for eng in ENGINES:
            P.wait(eng, "dc", dc_total)

        w32 = [W32A, W32B]
        s32 = ["s32a", "s32b"]
        out_guard = [0, 0]
        dve_guard = [0, 0]
        i = 0
        for nm, wb, ng, gm, nk in KINDS:
            for g in range(ng):
                j = i % 2
                P.wait("sync", "dve", dve_guard[j])
                src = wb[gm * g:gm * (g + 1)].rearrange("m k p f -> p m k f")
                dst = w32[j][:].rearrange("p (m k f) -> p m k f", m=gm, k=nk)
                P.op("sync", lambda sy, d=dst, sr=src: sy.dma_start(out=d, in_=sr),
                     inc=s32[j], n=16)
                P.wait("vector", s32[j], P.mark(s32[j]))
                P.wait("vector", "dscr", out_guard[j])
                slot = [WA, WB][j]
                vop(lambda v, s_=w32[j], d_=slot: v.tensor_copy(d_[:, 0:2048], s_))
                P.wait("vector", "dve", P.mark("dve"))
                vop(lambda v, s_=w32[j], d_=slot:
                    v.tensor_tensor(d_[:, 2048:4096], s_, d_[:, 0:2048], ALU.subtract))
                dve_guard[j] = P.mark("dve")
                P.wait("sync", "dve", dve_guard[j])
                sdst = scr[nm][g].rearrange("hl p f -> p hl f")
                ssrc = slot[:].rearrange("p (hl f) -> p hl f", hl=2)
                P.op("sync", lambda sy, d=sdst, sr=ssrc: sy.dma_start(out=d, in_=sr),
                     inc="dscr", n=16)
                out_guard[j] = P.mark("dscr")
                i += 1
        P.wait("sync", "dscr", P.mark("dscr"))
        P.wait("sync", "dve", P.mark("dve"))
        P.wait("gpsimd", "dve", P.mark("dve"))

        st = dict(xs_guard=0, ctail_mark=0, ctxf_free=0,
                  ytm_inflight=[0, 0], xrf_mark=0, dx_target=0)

        def emit_xloads(s, t):
            P.wait("gpsimd", "pe", st["xs_guard"])
            base = T * t
            for c in range(4):
                P.op("gpsimd",
                     lambda g, c_=1024 * c, t0=base + 128 * c, s_=s:
                     g.dma_start(out=XS[0:128, c_:c_ + 1024],
                                 in_=xin[s_, t0:t0 + 128, :]),
                     inc="dx", n=16)
            st["dx_target"] = P.mark("dx")

        def emit_tile(s, t, nxt):
            base = T * t

            for g in range(12):
                ws.push(scr["qkv"], g)
            for g in range(4):
                ws.push(scr["proj"], g)
            for g in range(4):
                ws.push(scr["pw"], g)
            for g in range(32):
                ws.push(scr["exp"], g)
            for g in range(4):
                ws.push(scr["out"], g)
            ws.push(scr["ffd"], 0)
            ws.push(scr["ffu"], 0)
            ws.prime()

            def xload(cols, tok0, ntok, poff=0):
                P.op("gpsimd",
                     lambda g, c=cols, t0=tok0, nt_=ntok, po=poff, s_=s:
                     g.dma_start(out=XS[po:po + nt_, c:c + 1024],
                                 in_=xin[s_, t0:t0 + nt_, :]),
                     inc="dx", n=16)

            P.wait("gpsimd", "dy0", st["ytm_inflight"][0])
            t4 = (base + 512) % N
            P.op("gpsimd",
                 lambda g, t0=t4, s_=s: g.dma_start(out=VTY[0:4, :],
                                                    in_=xin[s_, t0:t0 + 4, :]),
                 inc="dx", n=16)
            dx_target = P.mark("dx")

            P.wait("tensor", "dx", max(dx_target, st["dx_target"]))

            def xpose_group(cols_in, np_, dstcol, bank_i, src_t=None):
                src_t = XS if src_t is None else src_t
                for g in range(2):
                    bk = banks[bank_i + g]
                    bk.acquire()
                    for dd in range(4):
                        d = 4 * g + dd
                        tp(pb[bank_i + g][0:128, dd * 128:dd * 128 + np_],
                           src_t[0:np_, cols_in + 128 * d:cols_in + 128 * (d + 1)],
                           IDN[0:np_, 0:np_], inc=(dd == 3))
                    bk.written()
                    bk.rwait("vector")
                    dsl = XRF[:].rearrange("p (c f) -> p c f", c=CH)[:, 4 * g:4 * g + 4,
                                                                    dstcol:dstcol + np_]
                    ssl = pb[bank_i + g][:].rearrange("p (c f) -> p c f", c=4)[:, :, 0:np_]
                    vop(lambda v, d_=dsl, s_=ssl: v.tensor_copy(d_, s_))
                    bk.radd("dve")

            for c in range(4):
                xpose_group(1024 * c, 128, 128 * c, (2 * c) % 8)
            xpose_group(0, 4, 512, 0, src_t=VTY)
            if t == 0:
                P.wait("gpsimd", "pe", P.mark("pe"))
                xload(0, N - 4, 4, poff=0)
                xload(0, 0, 4, poff=4)
                P.wait("tensor", "dx", P.mark("dx"))
                xpose_group(0, 8, 516, 2)
            st["xs_guard"] = P.mark("pe")
            st["xrf_mark"] = P.mark("dve")
            tap("xrf", XRF[:], "dve", st["xrf_mark"])
            if nxt is not None:
                emit_xloads(*nxt)
            if UPTO <= 1:
                return

            P.wait("vector", "dve", st["xrf_mark"])
            wwid = 524 if t == 0 else 516
            sp_wave([(XRF[:, XR_C * c:XR_C * c + wwid],
                      XRFh[:, XR_C * c:XR_C * c + wwid],
                      XRFl[:, XR_C * c:XR_C * c + wwid]) for c in range(CH)])
            xrhl_mark = P.mark("dve")
            P.wait("tensor", "dve", xrhl_mark)
            slot = None
            for m in range(24):
                if m % 2 == 0:
                    if slot is not None:
                        ws.done(slot)
                    slot = ws.consume()
                if m < 8:
                    dsth = QFh[:, QK_C * m:QK_C * m + 520]
                    dstl = QFl[:, QK_C * m:QK_C * m + 520]
                    ev = lambda v, d, p, c: v.tensor_scalar(
                        d, p, BQ[:, c:c + 1], 0.125, ALU.add, ALU.mult)
                elif m < 16:
                    dsth = KFh[:, QK_C * (m - 8):QK_C * (m - 8) + 520]
                    dstl = KFl[:, QK_C * (m - 8):QK_C * (m - 8) + 520]
                    ev = lambda v, d, p, c: v.tensor_scalar(
                        d, p, BQ[:, c:c + 1], None, ALU.add)
                else:
                    dsth = None
                    dstl = VF[:, QK_C * (m - 16):QK_C * (m - 16) + 520]
                    ev = lambda v, d, p, c: v.tensor_scalar(
                        d, p, BQ[:, c:c + 1], None, ALU.add)
                bk = banks[m % 4]
                bk.acquire()
                mm3(pb[m % 4][:], lambda hl, k, s_=slot: ws.block(s_, hl, m % 2, k, 8),
                    lambda k: XRFh[:, XR_C * k + 4:XR_C * k + 516],
                    lambda k: XRFl[:, XR_C * k + 4:XR_C * k + 516], CH)
                bk.written()
                bk.rwait("vector")
                if m >= 16:
                    vop(lambda v, d=dstl[:, 0:512], p=pb[m % 4], c=m, e_=ev:
                        e_(v, d, p[:], c))
                else:
                    sc_h = SC[:, 512 * (m % 2):512 * (m % 2) + 512]
                    vop(lambda v, d=sc_h, p=pb[m % 4], c=m, e_=ev: e_(v, d, p[:], c))
                    P.wait("vector", "dve", P.mark("dve"))
                    sp_evac(sc_h, dsth[:, 0:512], dstl[:, 0:512])
                bk.radd("dve")
                if t == 0:
                    bkw = banks[4 + m % 2]
                    bkw.acquire()
                    mm3(pb[4 + m % 2][0:128, 0:8],
                        lambda hl, k, s_=slot: ws.block(s_, hl, m % 2, k, 8),
                        lambda k: XRFh[:, XR_C * k + 516:XR_C * k + 524],
                        lambda k: XRFl[:, XR_C * k + 516:XR_C * k + 524], CH)
                    bkw.written()
                    bkw.rwait("vector")
                    if m >= 16:
                        vop(lambda v, d=dstl[:, 512:520], p=pb[4 + m % 2], c=m, e_=ev:
                            e_(v, d, p[0:128, 0:8], c))
                    else:
                        scw = SM[:, 56:64]
                        vop(lambda v, d=scw, p=pb[4 + m % 2], c=m, e_=ev:
                            e_(v, d, p[0:128, 0:8], c))
                        P.wait("vector", "dve", P.mark("dve"))
                        sp_evac(scw, dsth[:, 512:520], dstl[:, 512:520])
                    bkw.radd("dve")
            if slot is not None:
                ws.done(slot)

            qkv_mark = P.mark("dve")
            if UPTO <= 2:
                return
            tap("qf", QFh[:], "dve", qkv_mark)
            tap("kf", KFh[:], "dve", qkv_mark)
            tap("vf", VF[:], "dve", qkv_mark)

            P.wait("tensor", "dve", qkv_mark)

            def attn_block(qcol, ctx0, W):
                idw = IDN[0:W, 0:W]
                for g in range(2):
                    bk = banks[6 + g]
                    bk.acquire()
                    for dd in range(4):
                        d = 4 * g + dd
                        tp(pb[6 + g][0:W, dd * 128:(dd + 1) * 128],
                           VF[:, QK_C * d + qcol:QK_C * d + qcol + W],
                           IDN[:, :], inc=(dd == 3))
                    bk.written()
                    bk.rwait("vector")
                    sp_evac(pb[6 + g][0:W, :], VTh[0:W, 512 * g:512 * (g + 1)],
                            VTl[0:W, 512 * g:512 * (g + 1)])
                    bk.radd("dve")
                vt_mark = P.mark("dve")
                if ATTN_SUB <= 1:
                    return
                sbank = lambda h: 2 * (h % 2) + (h // 8)
                squar = lambda h: (h // 2) % 4
                for b_ in range(4):
                    banks[b_].acquire()
                for h in range(NH):
                    pp = 64 * (h % 2)
                    tpos = (64, 0) if h % 2 else None
                    qsl = lambda T_, h_=h: T_[64 * (h_ % 2):64 * (h_ % 2) + 64,
                                              QK_C * (h_ // 2) + qcol:
                                              QK_C * (h_ // 2) + qcol + W]
                    for i3, (lh, rh) in enumerate(((QFh, KFh), (QFl, KFh), (QFh, KFl))):
                        mm(pb[sbank(h)][0:W, squar(h) * 128:squar(h) * 128 + W],
                           qsl(lh), qsl(rh), start=(i3 == 0), stop=(i3 == 2),
                           inc=(h == NH - 1 and i3 == 2), tile_position=tpos)
                for b_ in range(4):
                    banks[b_].written()
                if ATTN_SUB <= 2:
                    return
                for h in range(NH):
                    banks[sbank(h)].rwait("vector")
                    vop(lambda v, h_=h: v.tensor_tensor(
                        P0[0:W, 128 * h_:128 * h_ + W],
                        pb[sbank(h_)][0:W, squar(h_) * 128:squar(h_) * 128 + W],
                        MSK[0:W, 128 * h_:128 * h_ + W], ALU.add))
                    banks[sbank(h)].radd("dve")
                m1 = P.mark("dve")
                P.wait("vector", "dve", m1)
                for h in range(NH):
                    vop(lambda v, h_=h: v.tensor_reduce(
                        NS[0:W, h_:h_ + 1], P0[0:W, 128 * h_:128 * h_ + W],
                        axis=AX, op=ALU.max, negate=True))
                m2 = P.mark("dve")
                P.wait("scalar", "dve", m2)
                for h in range(NH):
                    aop(lambda sc, h_=h: sc.activation(
                        P0[0:W, 128 * h_:128 * h_ + W], P0[0:W, 128 * h_:128 * h_ + W],
                        AF.Exp, bias=NS[0:W, h_:h_ + 1],
                        accum_out=NS[0:W, 16 + h_:17 + h_]))
                m3 = P.mark("act")
                P.wait("vector", "act", m3)
                vop(lambda v: v.reciprocal(NS[0:W, 32:48], NS[0:W, 16:32]))
                rinv_mark = P.mark("dve")
                if ATTN_SUB <= 3:
                    return
                P.wait("tensor", "act", m3)
                for h in range(NH):
                    bk = banks[4 + h // 4]
                    if h % 4 == 0:
                        bk.acquire()
                    tp(pb[4 + h // 4][0:W, (h % 4) * 128:(h % 4) * 128 + W],
                       P0[0:W, 128 * h:128 * h + W], idw, inc=(h % 4 == 3))
                    if h % 4 == 3:
                        bk.written()
                for g in range(4):
                    banks[4 + g].rwait("vector")
                    sp_evac(pb[4 + g][0:W, :], PTSh[0:W, 512 * g:512 * (g + 1)],
                            PTSl[0:W, 512 * g:512 * (g + 1)])
                    banks[4 + g].radd("dve")
                pts_mark = P.mark("dve")
                if ATTN_SUB <= 4:
                    return
                P.wait("tensor", "dve", pts_mark)
                for h in range(NH):
                    bk = banks[h // 8]
                    if h % 8 == 0:
                        bk.acquire()
                    for i3, (lh, rh) in enumerate(((PTSh, VTh), (PTSl, VTh), (PTSh, VTl))):
                        mm(pb[h // 8][0:W, 64 * (h % 8):64 * (h % 8) + 64],
                           lh[0:W, 128 * h:128 * h + W], rh[0:W, 64 * h:64 * h + 64],
                           start=(i3 == 0), stop=(i3 == 2),
                           inc=(h % 8 == 7 and i3 == 2))
                    if h % 8 == 7:
                        bk.written()
                if ATTN_SUB <= 5:
                    return
                P.wait("vector", "dve", rinv_mark)
                for h in range(NH):
                    banks[h // 8].rwait("vector")
                    vop(lambda v, h_=h: v.tensor_scalar(
                        VT[0:W, 64 * h_:64 * h_ + 64],
                        pb[h_ // 8][0:W, 64 * (h_ % 8):64 * (h_ % 8) + 64],
                        NS[0:W, 32 + h_:33 + h_], None, ALU.mult))
                    banks[h // 8].radd("dve")
                ctm_mark = P.mark("dve")
                if ATTN_SUB <= 6:
                    return
                P.wait("tensor", "dve", ctm_mark)
                for g in range(2):
                    bk = banks[2 + g]
                    bk.acquire()
                    for cc_ in range(4):
                        c = 4 * g + cc_
                        tp(pb[2 + g][0:128, cc_ * 128:cc_ * 128 + W],
                           VT[0:W, 128 * c:128 * (c + 1)], idw, inc=(cc_ == 3))
                    bk.written()
                    bk.rwait("vector")
                    dslh = CTXFh[:].rearrange("p (c f) -> p c f", c=CH)[
                        :, 4 * g:4 * g + 4, ctx0:ctx0 + W]
                    dsll = CTXFl[:].rearrange("p (c f) -> p c f", c=CH)[
                        :, 4 * g:4 * g + 4, ctx0:ctx0 + W]
                    ssl = pb[2 + g][:].rearrange("p (c f) -> p c f", c=4)[:, :, 0:W]
                    sp_evac(ssl, dslh, dsll)
                    bk.radd("dve")

            P.wait("vector", "pe", st["ctxf_free"])
            P.wait("vector", "dy0", st["ytm_inflight"][0])
            P.wait("vector", "dy1", st["ytm_inflight"][1])
            if t == 0:
                if ATTN_PARTS is None or "mini" in ATTN_PARTS:
                    attn_block(512, 0, 8)
            else:
                if ATTN_PARTS is None or "tail" in ATTN_PARTS:
                    P.wait("vector", "dve", st["ctail_mark"])
                    for Td, Ts in ((CTXFh, CTAILh), (CTXFl, CTAILl)):
                        dsl = Td[:].rearrange("p (c f) -> p c f", c=CH)[:, :, 0:8]
                        ssl = Ts[:].rearrange("p (c f) -> p c f", c=CH)
                        vop(lambda v, d_=dsl, s_=ssl: v.tensor_copy(d_, s_))
            for b in range(NB):
                if ATTN_PARTS is None or ("b%d" % b) in ATTN_PARTS:
                    attn_block(128 * b, 8 + 128 * b, 128)
            for Td, Ts in ((CTAILh, CTXFh), (CTAILl, CTXFl)):
                dsl = Td[:].rearrange("p (c f) -> p c f", c=CH)
                ssl = Ts[:].rearrange("p (c f) -> p c f", c=CH)[:, :, 512:520]
                vop(lambda v, d_=dsl, s_=ssl: v.tensor_copy(d_, s_))
            st["ctail_mark"] = P.mark("dve")
            ctx_mark = P.mark("dve")
            tap("ctx", CTXFh[:], "dve", ctx_mark)
            if UPTO <= 3:
                return

            P.wait("tensor", "dve", ctx_mark)
            for g in range(4):
                slot = ws.consume()
                for ml in range(2):
                    m = 2 * g + ml
                    bk = banks[m % 4]
                    bk.acquire()
                    mm3(pb[m % 4][:],
                        lambda hl, k, s_=slot, ml_=ml: ws.block(s_, hl, ml_, k, 8),
                        lambda k: CTXFh[:, CT_C * k + 4:CT_C * k + 516],
                        lambda k: CTXFl[:, CT_C * k + 4:CT_C * k + 516], CH)
                    bk.written()
                    bk.rwait("vector")
                    sc_h = SC[:, 512 * (m % 2):512 * (m % 2) + 512]
                    vop(lambda v, d=sc_h, p=pb[m % 4], c=m:
                        v.tensor_scalar(d, p[:], BP[:, c:c + 1], None, ALU.add))
                    bk.radd("dve")
                    P.wait("vector", "dve", P.mark("dve"))
                    vop(lambda v, d=VF[:, QK_C * m:QK_C * m + 512], s_=sc_h, c=m:
                        v.tensor_tensor(d, s_, XRF[:, XR_C * c:XR_C * c + 512], ALU.add))
                ws.done(slot)
            rf_mark = P.mark("dve")
            tap("rf", VF[:], "dve", rf_mark)
            if UPTO <= 4:
                return

            h_dst = S1B[:].rearrange("p (c f) -> p c f", c=CH)[:, :, 0:2]
            if t == 0:
                vop(lambda v, d=h_dst: v.memset(d, 0.0))
            else:
                h_src = S1B[:].rearrange("p (c f) -> p c f", c=CH)[:, :, 512:514]
                vop(lambda v, d=h_dst, s_=h_src: v.tensor_copy(d, s_))
            halo_mark = P.mark("dve")

            def emit_norm(src_stride, src_t, dst_write, eps, subtract_mean,
                          read_guard_sem=None, read_guard=0):
                if read_guard_sem is not None:
                    P.wait("scalar", read_guard_sem, read_guard)
                    P.wait("tensor", read_guard_sem, read_guard)
                P.wait("scalar", "pe", P.mark("pe"))
                P.wait("scalar", "dve", P.mark("dve"))
                bkA, bkB = banks[4], banks[5]
                bkA.acquire()
                bkB.acquire()
                if subtract_mean:
                    for k in range(CH):
                        mm(pb[4][0:1, :], OC[:], src_t[:, src_stride * k:src_stride * k + 512],
                           start=(k == 0), stop=(k == CH - 1), inc=(k == CH - 1))
                bkA.written()
                half_guard = [0, 0]
                for c in range(CH):
                    if half_guard[c % 2]:
                        P.wait("scalar", "pe", half_guard[c % 2])
                    aop(lambda sc, c_=c: sc.activation(
                        SC[:, 512 * (c_ % 2):512 * (c_ % 2) + 512],
                        src_t[:, src_stride * c_:src_stride * c_ + 512], AF.Square))
                    P.wait("tensor", "act", P.mark("act"))
                    mm(pb[5][0:1, :], OC[:], SC[:, 512 * (c % 2):512 * (c % 2) + 512],
                       start=(c == 0), stop=(c == CH - 1), inc=True)
                    half_guard[c % 2] = P.mark("pe")
                bkB.written()
                seg = lambda i: STATS[0:1, 512 * i:512 * (i + 1)]
                bkB.rwait("vector")
                if subtract_mean:
                    bkA.rwait("vector")
                    vop(lambda v: v.tensor_scalar(seg(0), pb[4][0:1, :], 1.0 / D, None, ALU.mult))
                    bkA.radd("dve")
                vop(lambda v: v.tensor_scalar(seg(1), pb[5][0:1, :], 1.0 / D, None, ALU.mult))
                bkB.radd("dve")
                P.wait("vector", "dve", P.mark("dve"))
                if subtract_mean:
                    vop(lambda v: v.tensor_tensor(seg(2), seg(0), seg(0), ALU.mult))
                    P.wait("vector", "dve", P.mark("dve"))
                    vop(lambda v: v.tensor_tensor(seg(1), seg(1), seg(2), ALU.subtract))
                    P.wait("vector", "dve", P.mark("dve"))
                vop(lambda v, e=eps: v.tensor_scalar(seg(1), seg(1), 1.0, e, ALU.mult, ALU.add))
                vm = P.mark("dve")
                P.wait("scalar", "dve", vm)
                aop(lambda sc: sc.activation(seg(2), seg(1), AF.Ln))
                P.wait("scalar", "act", P.mark("act"))
                aop(lambda sc: sc.activation(seg(2), seg(2), AF.Exp, scale=-0.5))
                rim = P.mark("act")
                if subtract_mean:
                    P.wait("vector", "act", rim)
                    vop(lambda v: v.tensor_tensor(seg(3), seg(0), seg(2), ALU.mult))
                    P.wait("vector", "dve", P.mark("dve"))
                    vop(lambda v: v.tensor_scalar(seg(3), seg(3), -1.0, None, ALU.mult))
                nmr_mark = P.mark("dve")
                bkC, bkD = banks[6], banks[7]
                bkC.acquire()
                P.wait("tensor", "act", rim)
                mm(pb[6][:], OR[:], seg(2), start=True, stop=True, inc=True)
                bkC.written()
                if subtract_mean:
                    bkD.acquire()
                    P.wait("tensor", "dve", nmr_mark)
                    mm(pb[7][:], OR[:], seg(3), start=True, stop=True, inc=True)
                    bkD.written()
                bkC.rwait("vector")
                if subtract_mean:
                    bkD.rwait("vector")
                dst_write(bkC, bkD)
                bkC.radd("dve")
                if subtract_mean:
                    bkD.radd("dve")

            def ln1_write(bkC, bkD):
                P.wait("vector", "dve", halo_mark)
                for c in range(CH):
                    vop(lambda v, c_=c: v.tensor_tensor(
                        S1B[:, S1_C * c_ + 2:S1_C * c_ + 514],
                        VF[:, QK_C * c_:QK_C * c_ + 512], pb[6][:], ALU.mult))
                P.wait("vector", "dve", P.mark("dve"))
                for c in range(CH):
                    vop(lambda v, c_=c: v.tensor_tensor(
                        S1B[:, S1_C * c_ + 2:S1_C * c_ + 514],
                        S1B[:, S1_C * c_ + 2:S1_C * c_ + 514], pb[7][:], ALU.add))

            P.wait("vector", "dve", rf_mark)
            emit_norm(QK_C, VF, ln1_write, EPS_LN, True,
                      read_guard_sem="dve", read_guard=rf_mark)
            s1_mark = P.mark("dve")
            tap("s1", S1B[:], "dve", s1_mark)
            if UPTO <= 5:
                return

            P.wait("vector", "pe", P.mark("pe"))
            P.wait("vector", "dve", s1_mark)
            for c in range(CH):
                vop(lambda v, c_=c: v.tensor_scalar(
                    VF[:, QK_C * c_:QK_C * c_ + 512],
                    S1B[:, S1_C * c_ + 2:S1_C * c_ + 514],
                    DWK[:, 3 * c_ + 2:3 * c_ + 3], DWB[:, c_:c_ + 1], ALU.mult, ALU.add))
            for tap_i, off in ((1, 1), (0, 0)):
                for hf in range(2):
                    cs = range(4 * hf, 4 * hf + 4)
                    for c in cs:
                        vop(lambda v, c_=c, o=off, ti=tap_i: v.tensor_scalar(
                            P0[:, 512 * (c_ % 4):512 * (c_ % 4) + 512],
                            S1B[:, S1_C * c_ + o:S1_C * c_ + o + 512],
                            DWK[:, 3 * c_ + ti:3 * c_ + ti + 1], None, ALU.mult))
                    P.wait("vector", "dve", P.mark("dve"))
                    for c in cs:
                        vop(lambda v, c_=c: v.tensor_tensor(
                            VF[:, QK_C * c_:QK_C * c_ + 512],
                            VF[:, QK_C * c_:QK_C * c_ + 512],
                            P0[:, 512 * (c_ % 4):512 * (c_ % 4) + 512], ALU.add))
                    P.wait("vector", "dve", P.mark("dve"))
            sp_wave([(VF[:, QK_C * c:QK_C * c + 512],
                      QFh[:, QK_C * c:QK_C * c + 512],
                      QFl[:, QK_C * c:QK_C * c + 512]) for c in range(CH)])
            cv_mark = P.mark("dve")
            tap("cv", VF[:], "dve", cv_mark)
            if UPTO <= 6:
                return

            P.wait("tensor", "dve", cv_mark)
            for g in range(4):
                slot = ws.consume()
                for ml in range(2):
                    m = 2 * g + ml
                    bk = banks[m % 4]
                    bk.acquire()
                    mm3(pb[m % 4][:],
                        lambda hl, k, s_=slot, ml_=ml: ws.block(s_, hl, ml_, k, 8),
                        lambda k: QFh[:, QK_C * k:QK_C * k + 512],
                        lambda k: QFl[:, QK_C * k:QK_C * k + 512], CH)
                    bk.written()
                    bk.rwait("scalar")
                    sc_h = SC[:, 512 * (m % 2):512 * (m % 2) + 512]
                    aop(lambda sc, d=sc_h, p=pb[m % 4], c=m:
                        sc.activation(d, p[:], AF.Silu, bias=PWB[:, c:c + 1]))
                    bk.radd("act")
                    P.wait("vector", "act", P.mark("act"))
                    vop(lambda v, c_=m, s_=sc_h: v.tensor_tensor(
                        XRF[:, XR_C * c_:XR_C * c_ + 512],
                        S1B[:, S1_C * c_ + 2:S1_C * c_ + 514], s_, ALU.add))
                ws.done(slot)
            P.wait("vector", "dve", P.mark("dve"))
            sp_wave([(XRF[:, XR_C * c:XR_C * c + 512],
                      XRFh[:, XR_C * c:XR_C * c + 512],
                      XRFl[:, XR_C * c:XR_C * c + 512]) for c in range(CH)])
            x1_mark = P.mark("dve")
            tap("x1", XRF[:], "dve", x1_mark)
            if UPTO <= 7:
                return

            P.wait("tensor", "dve", x1_mark)
            for mt in range(4):
                bk = banks[4 + mt]
                bk.acquire()
                for k in range(CH):
                    mm(pb[4 + mt][0:128, 0:8],
                       XRF[:, XR_C * k + 128 * mt:XR_C * k + 128 * (mt + 1)],
                       RT[:, 8 * k:8 * k + 8],
                       start=(k == 0), stop=(k == CH - 1), inc=(k == CH - 1))
                bk.written()
                bk.rwait("vector")
                vop(lambda v, m_=mt: v.tensor_tensor(
                    LG[:, 8 * m_:8 * m_ + 8], pb[4 + m_][0:128, 0:8], RBB[:], ALU.add))
                bk.radd("dve")
            col = lambda t_, i: t_[:, i:i + 1]
            for mt in range(4):
                lg = LG[:, 8 * mt:8 * mt + 8]
                m0c, m1c = col(SM, mt), col(SM, 4 + mt)
                w0c, w1c = col(SM, 8 + mt), col(SM, 12 + mt)
                ec, e1c = col(SM, 16 + mt), col(SM, 20 + mt)
                eq0 = SM[:, 24 + 8 * mt:32 + 8 * mt]
                eq1 = RWV[:, 8 * mt:8 * mt + 8]

                def step(f):
                    vop(f)
                    P.wait("vector", "dve", P.mark("dve"))
                step(lambda v, a=m0c, b=lg: v.tensor_reduce(a, b, axis=AX, op=ALU.max))
                step(lambda v, a=eq0, b=lg, c=m0c: v.tensor_scalar(a, b, c, None, ALU.is_ge))
                step(lambda v, a=eq1, b=eq0: v.tensor_scalar(a, b, -1e30, None, ALU.mult))
                step(lambda v, a=eq1, b=lg: v.tensor_tensor(a, a, b, ALU.add))
                step(lambda v, a=m1c, b=eq1: v.tensor_reduce(a, b, axis=AX, op=ALU.max))
                step(lambda v, a=eq1, b=lg, c=m1c: v.tensor_scalar(a, b, c, None, ALU.is_ge))
                step(lambda v, a=eq1, b=eq0: v.tensor_tensor(a, a, b, ALU.subtract))
                step(lambda v, a=ec, b=m1c, c=m0c: v.tensor_tensor(a, b, c, ALU.subtract))
                P.wait("scalar", "dve", P.mark("dve"))
                aop(lambda sc, a=ec: sc.activation(a, a, AF.Exp))
                P.wait("vector", "act", P.mark("act"))
                step(lambda v, a=e1c, b=ec: v.tensor_scalar(a, b, 1.0, 1.0, ALU.mult, ALU.add))
                step(lambda v, a=w1c, b=e1c: v.reciprocal(a, b))
                step(lambda v, a=w1c, b=ec: v.tensor_tensor(a, a, b, ALU.mult))
                step(lambda v, a=w0c, b=w1c: v.tensor_scalar(a, b, -1.0, 1.0, ALU.mult, ALU.add))
                step(lambda v, a=eq0, b=w0c: v.tensor_scalar(a, a, b, None, ALU.mult))
                step(lambda v, a=eq1, b=w1c: v.tensor_scalar(a, a, b, None, ALU.mult))
                step(lambda v, a=eq1, b=eq0: v.tensor_tensor(a, a, b, ALU.add))
            rw_mark = P.mark("dve")
            tap("lg", LG[:], "dve", rw_mark)
            tap("rw", RWV[:], "dve", rw_mark)
            P.wait("tensor", "dve", rw_mark)
            bk = banks[4]
            bk.acquire()
            for mt in range(4):
                tp(pb[4][0:8, 128 * mt:128 * (mt + 1)], RWV[:, 8 * mt:8 * mt + 8],
                   IDN[:, :], inc=(mt == 3))
            bk.written()
            bk.rwait("vector")
            vop(lambda v: v.tensor_copy(RWF8[0:8, :], pb[4][0:8, 0:512]))
            bk.radd("dve")
            rwf_mark = P.mark("dve")
            if UPTO <= 8:
                return

            P.wait("tensor", "dve", rwf_mark)
            for e in range(E):
                rwbc_bank = 6 + e % 2
                nb_ = banks[rwbc_bank]
                nb_.acquire()
                mm(pb[rwbc_bank][:], SEL[0:8, 128 * e:128 * (e + 1)], RWF8[0:8, :],
                   start=True, stop=True, inc=True)
                nb_.written()
                for m in range(CH):
                    gi = e * 8 + m
                    if gi % 2 == 0:
                        slot = ws.consume()
                    bk = banks[gi % 4]
                    bk.acquire()
                    mm3(pb[gi % 4][:],
                        lambda hl, k, s_=slot, ml_=gi % 2: ws.block(s_, hl, ml_, k, 8),
                        lambda k: XRFh[:, XR_C * k:XR_C * k + 512],
                        lambda k: XRFl[:, XR_C * k:XR_C * k + 512], CH)
                    bk.written()
                    bk.rwait("scalar")
                    sc_h = SC[:, 512 * (m % 2):512 * (m % 2) + 512]
                    aop(lambda sc, d=sc_h, p=pb[gi % 4], e_=e, m_=m:
                        sc.activation(d, p[:], AF.Silu, bias=EB[:, 8 * e_ + m_:8 * e_ + m_ + 1]))
                    bk.radd("act")
                    am = P.mark("act")
                    P.wait("vector", "act", am)
                    banks[rwbc_bank].rwait("vector")
                    if e == 0:
                        vop(lambda v, d=COMB[:, 512 * m:512 * m + 512], s_=sc_h,
                            rb=pb[rwbc_bank]: v.tensor_tensor(d, s_, rb[:], ALU.mult))
                    else:
                        vop(lambda v, s_=sc_h, rb=pb[rwbc_bank], m_=m:
                            v.tensor_tensor(P0[:, 512 * (m_ % 2):512 * (m_ % 2) + 512],
                                            s_, rb[:], ALU.mult))
                        P.wait("vector", "dve", P.mark("dve"))
                        vop(lambda v, d=COMB[:, 512 * m:512 * m + 512], m_=m:
                            v.tensor_tensor(d, d, P0[:, 512 * (m_ % 2):512 * (m_ % 2) + 512],
                                            ALU.add))
                    banks[rwbc_bank].radd("dve")
                    if gi % 2 == 1:
                        ws.done(slot)
            P.wait("vector", "pe", P.mark("pe"))
            P.wait("vector", "dve", P.mark("dve"))
            sp_wave([(COMB[:, 512 * c:512 * c + 512],
                      KFh[:, QK_C * c:QK_C * c + 512],
                      KFl[:, QK_C * c:QK_C * c + 512]) for c in range(CH)])
            comb_mark = P.mark("dve")
            tap("comb", COMB[:], "dve", comb_mark)
            if UPTO <= 9:
                return

            P.wait("tensor", "dve", comb_mark)
            for g in range(4):
                slot = ws.consume()
                for ml in range(2):
                    m = 2 * g + ml
                    bk = banks[m % 4]
                    bk.acquire()
                    mm3(pb[m % 4][:],
                        lambda hl, k, s_=slot, ml_=ml: ws.block(s_, hl, ml_, k, 8),
                        lambda k: KFh[:, QK_C * k:QK_C * k + 512],
                        lambda k: KFl[:, QK_C * k:QK_C * k + 512], CH)
                    bk.written()
                    bk.rwait("vector")
                    sc_h = SC[:, 512 * (m % 2):512 * (m % 2) + 512]
                    vop(lambda v, d=sc_h, p=pb[m % 4], c=m:
                        v.tensor_scalar(d, p[:], OB[:, c:c + 1], None, ALU.add))
                    bk.radd("dve")
                    P.wait("vector", "dve", P.mark("dve"))
                    vop(lambda v, c=m, s_=sc_h: v.tensor_tensor(
                        XRF[:, XR_C * c:XR_C * c + 512],
                        XRF[:, XR_C * c:XR_C * c + 512], s_, ALU.add))
                ws.done(slot)
            hp_mark = P.mark("dve")
            tap("hp", XRF[:], "dve", hp_mark)
            if UPTO <= 10:
                return

            def rms_write(bkC, bkD):
                for c in range(CH):
                    vop(lambda v, c_=c: v.tensor_tensor(
                        VF[:, QK_C * c_:QK_C * c_ + 512],
                        XRF[:, XR_C * c_:XR_C * c_ + 512], pb[6][:], ALU.mult))
                P.wait("vector", "dve", P.mark("dve"))
                for c in range(CH):
                    vop(lambda v, c_=c: v.tensor_scalar(
                        VF[:, QK_C * c_:QK_C * c_ + 512],
                        VF[:, QK_C * c_:QK_C * c_ + 512],
                        NW[:, c_:c_ + 1], None, ALU.mult))

            P.wait("vector", "dve", hp_mark)
            P.wait("vector", "pe", P.mark("pe"))
            emit_norm(XR_C, XRF, rms_write, EPS_RMS, False,
                      read_guard_sem="dve", read_guard=hp_mark)
            P.wait("vector", "dve", P.mark("dve"))
            sp_wave([(VF[:, QK_C * c:QK_C * c + 512],
                      CTXFh[:, CT_C * c:CT_C * c + 512],
                      CTXFl[:, CT_C * c:CT_C * c + 512]) for c in range(CH)])
            h_mark = P.mark("dve")
            tap("h", VF[:], "dve", h_mark)
            if UPTO <= 11:
                return

            P.wait("tensor", "dve", h_mark)
            slot = ws.consume()
            for m in range(2):
                bk = banks[m]
                bk.acquire()
                mm3(pb[m][:],
                    lambda hl, k, s_=slot, m_=m: ws.block(s_, hl, m_, k, 8),
                    lambda k: CTXFh[:, CT_C * k:CT_C * k + 512],
                    lambda k: CTXFl[:, CT_C * k:CT_C * k + 512], CH)
                bk.written()
                bk.rwait("scalar")
                aop(lambda sc, d=FF32[:, 512 * m:512 * m + 512], p=pb[m], c=m:
                    sc.activation(d, p[:], AF.Gelu, bias=FDB[:, c:c + 1]))
                bk.radd("act")
            ws.done(slot)
            st["ctxf_free"] = P.mark("pe")
            ff1_mark = P.mark("act")
            P.wait("vector", "act", ff1_mark)
            P.wait("vector", "pe", P.mark("pe"))
            sp_wave([(FF32[:, 512 * m:512 * m + 512],
                      VTh[:, 512 * m:512 * m + 512],
                      VTl[:, 512 * m:512 * m + 512]) for m in range(2)])
            ff1_mark = P.mark("dve")
            tap("ff1", FF32[:], "dve", ff1_mark)
            if UPTO <= 12:
                return

            P.wait("tensor", "dve", ff1_mark)
            slot = ws.consume()
            for m in range(CH):
                bk = banks[m % 4]
                bk.acquire()
                mm3(pb[m % 4][:],
                    lambda hl, k, s_=slot, m_=m: ws.block(s_, hl, m_, k, 2),
                    lambda k: VTh[:, 512 * k:512 * k + 512],
                    lambda k: VTl[:, 512 * k:512 * k + 512], 2)
                bk.written()
                bk.rwait("vector")
                sc_h = SC[:, 512 * (m % 2):512 * (m % 2) + 512]
                vop(lambda v, d=sc_h, p=pb[m % 4], c=m:
                    v.tensor_scalar(d, p[:], FUB[:, c:c + 1], None, ALU.add))
                bk.radd("dve")
                P.wait("vector", "dve", P.mark("dve"))
                vop(lambda v, c=m, s_=sc_h: v.tensor_tensor(
                    XRF[:, XR_C * c:XR_C * c + 512], s_,
                    VF[:, QK_C * c:QK_C * c + 512], ALU.add))
                P.wait("vector", "dve", P.mark("dve"))
                vop(lambda v, c=m: v.tensor_tensor(
                    XRF[:, XR_C * c:XR_C * c + 512],
                    XRF[:, XR_C * c:XR_C * c + 512],
                    S1B[:, S1_C * c + 2:S1_C * c + 514], ALU.add))
            ws.done(slot)
            yp_mark = P.mark("dve")
            tap("yp", XRF[:], "dve", yp_mark)
            if UPTO <= 13:
                return

            def ln3_write(bkC, bkD):
                for c in range(CH):
                    vop(lambda v, c_=c: v.tensor_tensor(
                        VF[:, QK_C * c_:QK_C * c_ + 512],
                        XRF[:, XR_C * c_:XR_C * c_ + 512], pb[6][:], ALU.mult))
                P.wait("vector", "dve", P.mark("dve"))
                for c in range(CH):
                    vop(lambda v, c_=c: v.tensor_tensor(
                        VF[:, QK_C * c_:QK_C * c_ + 512],
                        VF[:, QK_C * c_:QK_C * c_ + 512], pb[7][:], ALU.add))

            P.wait("vector", "dve", yp_mark)
            emit_norm(XR_C, XRF, ln3_write, EPS_LN, True,
                      read_guard_sem="dve", read_guard=yp_mark)
            yf_mark = P.mark("dve")
            tap("yf", VF[:], "dve", yf_mark)
            if UPTO <= 14:
                return

            P.wait("tensor", "dve", yf_mark)
            for mt in range(4):
                for g in range(2):
                    bk = banks[2 * (mt % 2) + g]
                    bk.acquire()
                    for dd in range(4):
                        d = 4 * g + dd
                        tp(pb[2 * (mt % 2) + g][0:128, dd * 128:(dd + 1) * 128],
                           VF[:, QK_C * d + 128 * mt:QK_C * d + 128 * (mt + 1)],
                           IDN[:, :], inc=(dd == 3))
                    bk.written()
                    bk.rwait("vector")
                    P.wait("vector", "dy0", st["ytm_inflight"][0])
                    vop(lambda v, g_=g, b_=2 * (mt % 2) + g:
                        v.tensor_copy(VTY[:, 512 * g_:512 * (g_ + 1)], pb[b_][:]))
                    bk.radd("dve")
                em = P.mark("dve")
                P.wait("gpsimd", "dve", em)
                P.op("gpsimd",
                     lambda g_, s_=s, o=base + 128 * mt:
                     g_.dma_start(out=y[s_, o:o + 128, :], in_=VTY[:]),
                     inc="dy0", n=16)
                st["ytm_inflight"][0] = P.mark("dy0")

        ntiles = SEQS * NT if LIMIT_TILES is None else LIMIT_TILES
        seq = [(s, t) for s in range(SEQS) for t in range(NT)][:ntiles]
        emit_xloads(*seq[0])
        for i, (s, t) in enumerate(seq):
            emit_tile(s, t, seq[i + 1] if i + 1 < len(seq) else None)

        P.wait("gpsimd", "dy0", P.mark("dy0"))
        P.wait("gpsimd", "dy1", P.mark("dy1"))
        if taps:
            P.wait("gpsimd", "ddbg", P.mark("ddbg"))

        P.check_deadlock()
        with nc.Block() as block:
            P.replay(block)

    return nc


def _prep(inputs):
    g = {k: np.asarray(v) for k, v in inputs.items()}

    def blk(w, nm, nk):
        return np.ascontiguousarray(
            w.reshape(nk, 128, nm, 128).transpose(2, 0, 1, 3).astype(np.float32))

    out = {}
    out["wqkv_b"] = blk(g["L0_attn_w_qkv"], 24, 8)
    out["wproj_b"] = blk(g["L0_attn_w_proj"], 8, 8)
    out["wpw_b"] = blk(np.ascontiguousarray(g["L0_ffn_pw_w"].T), 8, 8)
    out["wexp_b"] = np.ascontiguousarray(
        np.stack([blk(np.ascontiguousarray(g["L0_ffn_expert_w"][e].T), 8, 8)
                  for e in range(8)]).reshape(64, 8, 128, 128))
    out["wout_b"] = blk(np.ascontiguousarray(g["L0_ffn_out_w"].T), 8, 8)
    out["wffd_b"] = blk(np.ascontiguousarray(g["L0_ffn_ff_down_w"].T), 2, 8)
    out["wffu_b"] = blk(np.ascontiguousarray(g["L0_ffn_ff_up_w"].T), 8, 2)

    def pmaj(a, *shape):
        a = np.asarray(a, np.float32).reshape(*shape)
        return np.ascontiguousarray(np.moveaxis(a, -1, 0).reshape(128, -1))

    bt = g["L0_attn_bias_table"].astype(np.float32)
    idx = 112 + (np.arange(8)[:, None] - np.arange(8)[None, :])
    bias8 = bt[idx.reshape(-1), :].reshape(8, 8, NH).transpose(2, 0, 1)
    M = np.full((NH, 128, 128), -1e9, np.float32)
    for w in range(16):
        M[:, 8 * w:8 * w + 8, 8 * w:8 * w + 8] = bias8
    out["mask_c"] = np.ascontiguousarray(M.transpose(1, 0, 2).reshape(128, NH * 128))
    out["ident_c"] = np.eye(128, dtype=np.float32)
    out["onescol_c"] = np.ones((128, 1), np.float32)
    out["onesrow_c"] = np.ones((1, 128), np.float32)
    out["sel_c"] = np.ascontiguousarray(
        np.kron(np.eye(8, dtype=np.float32), np.ones((1, 128), np.float32)))
    out["bqkv_c"] = pmaj(g["L0_attn_b_qkv"], 24, 128)
    out["bproj_c"] = pmaj(g["L0_attn_b_proj"], 8, 128)
    out["pwb_c"] = pmaj(g["L0_ffn_pw_b"], 8, 128)
    out["outb_c"] = pmaj(g["L0_ffn_out_b"], 8, 128)
    out["ffub_c"] = pmaj(g["L0_ffn_ff_up_b"], 8, 128)
    out["ffdb_c"] = pmaj(g["L0_ffn_ff_down_b"], 2, 128)
    out["nw_c"] = pmaj(g["L0_ffn_norm_w"], 8, 128)
    out["dwb_c"] = pmaj(g["L0_ffn_dw_bias"], 8, 128)
    dwk = np.asarray(g["L0_ffn_dw_kernel"], np.float32).reshape(8, 128, 3)
    out["dwk_c"] = np.ascontiguousarray(dwk.transpose(1, 0, 2).reshape(128, 24))
    rt = np.asarray(g["L0_ffn_router_w"], np.float32).T.reshape(8, 128, 8)
    out["rt_c"] = np.ascontiguousarray(rt.transpose(1, 0, 2).reshape(128, 64))
    out["rbbc_c"] = np.ascontiguousarray(
        np.broadcast_to(g["L0_ffn_router_b"].astype(np.float32)[None, :], (128, 8)))
    eb = np.asarray(g["L0_ffn_expert_b"], np.float32).reshape(8, 8, 128)
    out["eb_c"] = np.ascontiguousarray(eb.transpose(2, 0, 1).reshape(128, 64))
    return out


def kernel(**inputs):
    nc = build_nc()
    shared = _prep(inputs)
    x = np.ascontiguousarray(np.asarray(inputs["L0_attn_x"], dtype=np.float32))
    in_maps = []
    for c in range(NCORES):
        m = dict(shared)
        m["xin"] = np.ascontiguousarray(x[SEQS * c:SEQS * (c + 1)])
        in_maps.append(m)
    res = run_bass_kernel_spmd(nc, in_maps, core_ids=list(range(NCORES)))
    return np.concatenate([r["y"] for r in res.results], axis=0)


# version 04
# speedup vs baseline: 1.1096x; 1.0419x over previous
import contextlib
import numpy as np
import concourse.bass as bass
import concourse.mybir as mybir
from concourse.bass_utils import run_bass_kernel_spmd

F32 = mybir.dt.float32
F16 = mybir.dt.float16
AF = mybir.ActivationFunctionType
ALU = mybir.AluOpType
AX = mybir.AxisListType.X

NCORES = 8
SEQS = 2
N = 4096
D = 1024
CH = 8
T = 512
NT = N // T
NB = T // 128
NH = 16
WS = 8
E = 8
EPS_LN = 1e-5
EPS_RMS = 1e-6

XR_C = 524
QK_C = 520
CT_C = 520
S1_C = 514

ENGINES = ("tensor", "vector", "scalar", "sync", "gpsimd")

DEBUG_TAPS = set()
TAP_S = 0
TAP_T = 0
LIMIT_TILES = None
UPTO = 99
ATTN_PARTS = None
ATTN_SUB = 99


class Prog:

    def __init__(self, nc):
        self.nc = nc
        self.ops = {e: [] for e in ENGINES}
        self.sems = {}
        self.cnt = {}
        self.trace = {e: [] for e in ENGINES}

    def add_sem(self, name, handle):
        self.sems[name] = handle
        self.cnt[name] = 0

    def op(self, eng, fn, inc=None, n=1):
        if inc is not None:
            self.cnt[inc] += n
            sem = self.sems[inc]

            def run(e, fn=fn, sem=sem, n=n):
                fn(e).then_inc(sem, n)
        else:
            def run(e, fn=fn):
                fn(e)
        self.trace[eng].append(("inc", inc, n) if inc else ("inc", None, 0))
        self.ops[eng].append(run)
        return self.cnt[inc] if inc is not None else None

    def wait(self, eng, sem, val):
        assert val <= self.cnt[sem], f"wait {sem}>={val} but only {self.cnt[sem]} emitted"
        if val <= 0:
            return
        s = self.sems[sem]
        self.trace[eng].append(("wait", sem, val))
        self.ops[eng].append(lambda e, s=s, v=val: e.wait_ge(s, v))

    def check_deadlock(self):
        pos = {e: 0 for e in ENGINES}
        cur = {s: 0 for s in self.cnt}
        progress = True
        while progress:
            progress = False
            for e in ENGINES:
                tr = self.trace[e]
                while pos[e] < len(tr):
                    kind, sem, v = tr[pos[e]]
                    if kind == "wait" and cur[sem] < v:
                        break
                    if kind == "inc" and sem is not None:
                        cur[sem] += v
                    pos[e] += 1
                    progress = True
        stuck = {e: (pos[e], len(self.trace[e])) for e in ENGINES
                 if pos[e] < len(self.trace[e])}
        if stuck:
            detail = {e: self.trace[e][pos[e]] for e in stuck}
            raise RuntimeError(f"deadlock: stuck engines {stuck} at {detail}")

    def mark(self, sem):
        return self.cnt[sem]

    def replay(self, block):
        for eng in ENGINES:
            ops = self.ops[eng]
            if not ops:
                continue

            def body(handle, ops=ops):
                for f in ops:
                    f(handle)
            getattr(block, eng)(body)


class Bank:

    def __init__(self, prog, ap):
        self.P = prog
        self.ap = ap
        self.wmark = 0
        self.readers = []

    def acquire(self):
        for sem, val in self.readers:
            self.P.wait("tensor", sem, val)
        self.readers = []

    def written(self):
        self.wmark = self.P.mark("pe")

    def rwait(self, eng):
        self.P.wait(eng, "pe", self.wmark)

    def radd(self, sem):
        self.readers.append((sem, self.P.mark(sem)))


class WStream:

    def __init__(self, prog, slots, semnames):
        self.P = prog
        self.slots = slots
        self.sems = semnames
        self.q = []
        self.issued = []
        self.guard = [0, 0]
        self.nxt = 0

    def push(self, scr_ap, g):
        self.q.append((scr_ap, g))

    def _issue_one(self):
        scr_ap, g = self.q.pop(0)
        s = self.nxt
        self.nxt ^= 1
        self.P.wait("sync", "pe", self.guard[s])
        dst = self.slots[s][:].rearrange("p (hl f) -> p hl f", hl=2)
        src = scr_ap[g].rearrange("hl p f -> p hl f")
        self.P.op("sync", lambda sy, d=dst, sr=src: sy.dma_start(out=d, in_=sr),
                  inc=self.sems[s], n=16)
        self.issued.append((s, self.P.mark(self.sems[s])))

    def prime(self):
        while len(self.issued) < 2 and self.q:
            self._issue_one()

    def consume(self):
        s, target = self.issued.pop(0)
        self.P.wait("tensor", self.sems[s], target)
        return s

    def block(self, s, hl, m_local, k, nk):
        i = 2048 * hl + (m_local * nk + k) * 128
        return self.slots[s][:, i:i + 128]

    def done(self, s):
        self.guard[s] = self.P.mark("pe")
        if self.q:
            self._issue_one()


def build_nc():
    nc = bass.Bass("TRN2", target_bir_lowering=False)

    xin = nc.dram_tensor("xin", [SEQS, N, D], F32, kind="ExternalInput").ap()
    y = nc.dram_tensor("y", [SEQS, N, D], F32, kind="ExternalOutput").ap()

    wqkv_b = nc.dram_tensor("wqkv_b", [24, 8, 128, 128], F32, kind="ExternalInput").ap()
    wproj_b = nc.dram_tensor("wproj_b", [8, 8, 128, 128], F32, kind="ExternalInput").ap()
    wpw_b = nc.dram_tensor("wpw_b", [8, 8, 128, 128], F32, kind="ExternalInput").ap()
    wexp_b = nc.dram_tensor("wexp_b", [64, 8, 128, 128], F32, kind="ExternalInput").ap()
    wout_b = nc.dram_tensor("wout_b", [8, 8, 128, 128], F32, kind="ExternalInput").ap()
    wffd_b = nc.dram_tensor("wffd_b", [2, 8, 128, 128], F32, kind="ExternalInput").ap()
    wffu_b = nc.dram_tensor("wffu_b", [8, 2, 128, 128], F32, kind="ExternalInput").ap()

    KINDS = [
        ("qkv", wqkv_b, 12, 2, 8),
        ("proj", wproj_b, 4, 2, 8),
        ("pw", wpw_b, 4, 2, 8),
        ("exp", wexp_b, 32, 2, 8),
        ("out", wout_b, 4, 2, 8),
        ("ffd", wffd_b, 1, 2, 8),
        ("ffu", wffu_b, 1, 8, 2),
    ]
    scr = {}
    for nm, _, ng, _, _ in KINDS:
        scr[nm] = nc.dram_tensor("scr_" + nm, [ng, 2, 128, 2048], F16).ap()

    mask_c = nc.dram_tensor("mask_c", [128, NH * 128], F32, kind="ExternalInput").ap()
    ident_c = nc.dram_tensor("ident_c", [128, 128], F32, kind="ExternalInput").ap()
    onescol_c = nc.dram_tensor("onescol_c", [128, 1], F32, kind="ExternalInput").ap()
    onesrow_c = nc.dram_tensor("onesrow_c", [1, 128], F32, kind="ExternalInput").ap()
    sel_c = nc.dram_tensor("sel_c", [8, 8 * 128], F32, kind="ExternalInput").ap()
    bqkv_c = nc.dram_tensor("bqkv_c", [128, 24], F32, kind="ExternalInput").ap()
    bproj_c = nc.dram_tensor("bproj_c", [128, 8], F32, kind="ExternalInput").ap()
    pwb_c = nc.dram_tensor("pwb_c", [128, 8], F32, kind="ExternalInput").ap()
    outb_c = nc.dram_tensor("outb_c", [128, 8], F32, kind="ExternalInput").ap()
    ffub_c = nc.dram_tensor("ffub_c", [128, 8], F32, kind="ExternalInput").ap()
    ffdb_c = nc.dram_tensor("ffdb_c", [128, 2], F32, kind="ExternalInput").ap()
    nw_c = nc.dram_tensor("nw_c", [128, 8], F32, kind="ExternalInput").ap()
    dwb_c = nc.dram_tensor("dwb_c", [128, 8], F32, kind="ExternalInput").ap()
    dwk_c = nc.dram_tensor("dwk_c", [128, 24], F32, kind="ExternalInput").ap()
    rt_c = nc.dram_tensor("rt_c", [128, 64], F32, kind="ExternalInput").ap()
    rbbc_c = nc.dram_tensor("rbbc_c", [128, 8], F32, kind="ExternalInput").ap()
    eb_c = nc.dram_tensor("eb_c", [128, 64], F32, kind="ExternalInput").ap()

    taps = {}
    for tag, w in [("xrf", CH * XR_C), ("qf", CH * QK_C), ("kf", CH * QK_C),
                   ("vf", CH * QK_C), ("ctx", CH * CT_C), ("rf", CH * QK_C),
                   ("s1", CH * S1_C), ("cv", CH * QK_C), ("x1", CH * XR_C),
                   ("lg", 32), ("rw", 32), ("comb", CH * 512), ("hp", CH * XR_C),
                   ("h", CH * QK_C), ("ff1", 2 * 512), ("yp", CH * XR_C),
                   ("yf", CH * QK_C)]:
        if tag in DEBUG_TAPS:
            taps[tag] = nc.dram_tensor("dbg_" + tag, [128, w], F32,
                                       kind="ExternalOutput").ap()

    with contextlib.ExitStack() as ctx:
        def sbuf(name, shape):
            return ctx.enter_context(nc.sbuf_tensor(name, shape, F32))

        def sbuf16(name, shape):
            return ctx.enter_context(nc.sbuf_tensor(name, shape, F16))

        XS = sbuf("XS", [128, 4 * 1024])
        XRF = sbuf("XRF", [128, CH * XR_C])
        VF = sbuf("VF", [128, CH * QK_C])
        COMB = sbuf("COMB", [128, CH * 512])
        S1B = sbuf("S1B", [128, CH * S1_C])
        P0 = sbuf("P0", [128, NH * 128])
        VTY = P0[:, 0:1024]
        VT = sbuf("VT", [128, 1024])
        SC = sbuf("SC", [128, 2 * 512])
        FF32 = SC
        XRFh = sbuf16("XRFh", [128, CH * XR_C])
        XRFl = sbuf16("XRFl", [128, CH * XR_C])
        QFh = sbuf16("QFh", [128, CH * QK_C])
        QFl = sbuf16("QFl", [128, CH * QK_C])
        KFh = sbuf16("KFh", [128, CH * QK_C])
        KFl = sbuf16("KFl", [128, CH * QK_C])
        CTXFh = sbuf16("CTXFh", [128, CH * CT_C])
        CTXFl = sbuf16("CTXFl", [128, CH * CT_C])
        PTSh = XRFh[:, 0:NH * 128]
        PTSl = XRFl[:, 0:NH * 128]
        VTh = sbuf16("VTh", [128, 1024])
        VTl = sbuf16("VTl", [128, 1024])
        CTAILh = sbuf16("CTAILh", [128, CH * 8])
        CTAILl = sbuf16("CTAILl", [128, CH * 8])
        LG = sbuf("LG", [128, 32])
        SM = sbuf("SM", [128, 64])
        RWV = sbuf("RWV", [128, 32])
        RWF8 = sbuf("RWF8", [8, 512])
        SEL = sbuf("SEL", [8, 8 * 128])
        STATS = sbuf("STATS", [1, 4 * 512])
        NS = sbuf("NS", [128, 48])
        MSK = sbuf("MSK", [128, NH * 128])
        IDN = sbuf("IDN", [128, 128])
        OC = sbuf("OC", [128, 1])
        OR = sbuf("OR", [1, 128])
        BQ = sbuf("BQ", [128, 24])
        BP = sbuf("BP", [128, 8])
        PWB = sbuf("PWB", [128, 8])
        OB = sbuf("OB", [128, 8])
        FUB = sbuf("FUB", [128, 8])
        FDB = sbuf("FDB", [128, 2])
        NW = sbuf("NW", [128, 8])
        DWB = sbuf("DWB", [128, 8])
        DWK = sbuf("DWK", [128, 24])
        RT = sbuf("RT", [128, 64])
        RBB = sbuf("RBB", [128, 8])
        EB = sbuf("EB", [128, 64])
        WA = sbuf16("WA", [128, 4096])
        WB = sbuf16("WB", [128, 4096])
        W32A = COMB[:, 0:2048]
        W32B = P0[:, 0:2048]

        pb = [ctx.enter_context(nc.psum_tensor(f"pb{i}", [128, 512], F32))
              for i in range(8)]

        P = Prog(nc)
        for sname in ("pe", "dve", "act", "gp", "dwa", "dwb", "dx", "dy0",
                      "dy1", "dc", "ddbg", "s32a", "s32b", "dscr"):
            P.add_sem(sname, ctx.enter_context(nc.semaphore(sname)))
        banks = [Bank(P, t) for t in pb]
        ws = WStream(P, [WA, WB], ["dwa", "dwb"])

        def vop(fn):
            return P.op("vector", fn, inc="dve")

        def aop(fn):
            return P.op("scalar", fn, inc="act")

        def mm(out_ap, lhs_ap, rhs_ap, start, stop, inc=False, tile_position=None):
            def f(t, o=out_ap, l=lhs_ap, r=rhs_ap, st=start, sp=stop, tp=tile_position):
                if tp is not None:
                    return t.matmul(o, l, r, start=st, stop=sp, tile_position=tp)
                return t.matmul(o, l, r, start=st, stop=sp)
            return P.op("tensor", f, inc="pe" if inc else None)

        def tp(out_ap, in_ap, id_ap, inc=False):
            return P.op("tensor",
                        lambda t, o=out_ap, i=in_ap, d=id_ap: t.transpose(o, i, d),
                        inc="pe" if inc else None)

        def mm3(bank_ap, blkfn, rh, rl, nk, tpos=None):
            i, n = 0, 3 * nk
            for hl, rf in ((0, rh), (1, rh), (0, rl)):
                for k in range(nk):
                    mm(bank_ap, blkfn(hl, k), rf(k),
                       start=(i == 0), stop=(i == n - 1), inc=(i == n - 1),
                       tile_position=tpos)
                    i += 1

        def sp_evac(src32, dh, dl):
            vop(lambda v, s_=src32, d_=dh: v.tensor_copy(d_, s_))
            P.wait("vector", "dve", P.mark("dve"))
            vop(lambda v, s_=src32, h_=dh, d_=dl:
                v.tensor_tensor(d_, s_, h_, ALU.subtract))

        def sp_wave(triples):
            for src32, dh, dl in triples:
                vop(lambda v, s_=src32, d_=dh: v.tensor_copy(d_, s_))
            P.wait("vector", "dve", P.mark("dve"))
            for src32, dh, dl in triples:
                vop(lambda v, s_=src32, h_=dh, d_=dl:
                    v.tensor_tensor(d_, s_, h_, ALU.subtract))

        def tap(tag, src_ap, prod_sem, prod_val):
            if tag not in taps:
                return
            P.wait("gpsimd", prod_sem, prod_val)
            P.op("gpsimd", lambda g, o=taps[tag], s=src_ap: g.dma_start(out=o, in_=s),
                 inc="ddbg", n=16)

        def cload(dst, src):
            P.op("sync", lambda sy, d=dst, s=src: sy.dma_start(out=d, in_=s),
                 inc="dc", n=16)

        cload(MSK[:], mask_c)
        cload(IDN[:], ident_c)
        cload(OC[:], onescol_c)
        cload(OR[:], onesrow_c)
        cload(SEL[:], sel_c)
        cload(BQ[:], bqkv_c)
        cload(BP[:], bproj_c)
        cload(PWB[:], pwb_c)
        cload(OB[:], outb_c)
        cload(FUB[:], ffub_c)
        cload(FDB[:], ffdb_c)
        cload(NW[:], nw_c)
        cload(DWB[:], dwb_c)
        cload(DWK[:], dwk_c)
        cload(RT[:], rt_c)
        cload(RBB[:], rbbc_c)
        cload(EB[:], eb_c)
        dc_total = P.mark("dc")
        for eng in ENGINES:
            P.wait(eng, "dc", dc_total)

        w32 = [W32A, W32B]
        s32 = ["s32a", "s32b"]
        out_guard = [0, 0]
        dve_guard = [0, 0]
        i = 0
        for nm, wb, ng, gm, nk in KINDS:
            for g in range(ng):
                j = i % 2
                P.wait("sync", "dve", dve_guard[j])
                src = wb[gm * g:gm * (g + 1)].rearrange("m k p f -> p m k f")
                dst = w32[j][:].rearrange("p (m k f) -> p m k f", m=gm, k=nk)
                P.op("sync", lambda sy, d=dst, sr=src: sy.dma_start(out=d, in_=sr),
                     inc=s32[j], n=16)
                P.wait("vector", s32[j], P.mark(s32[j]))
                P.wait("vector", "dscr", out_guard[j])
                slot = [WA, WB][j]
                vop(lambda v, s_=w32[j], d_=slot: v.tensor_copy(d_[:, 0:2048], s_))
                P.wait("vector", "dve", P.mark("dve"))
                vop(lambda v, s_=w32[j], d_=slot:
                    v.tensor_tensor(d_[:, 2048:4096], s_, d_[:, 0:2048], ALU.subtract))
                dve_guard[j] = P.mark("dve")
                P.wait("sync", "dve", dve_guard[j])
                sdst = scr[nm][g].rearrange("hl p f -> p hl f")
                ssrc = slot[:].rearrange("p (hl f) -> p hl f", hl=2)
                P.op("sync", lambda sy, d=sdst, sr=ssrc: sy.dma_start(out=d, in_=sr),
                     inc="dscr", n=16)
                out_guard[j] = P.mark("dscr")
                i += 1
        P.wait("sync", "dscr", P.mark("dscr"))
        P.wait("sync", "dve", P.mark("dve"))
        P.wait("gpsimd", "dve", P.mark("dve"))

        st = dict(xs_guard=0, ctail_mark=0, ctxf_free=0,
                  ytm_inflight=[0, 0], xrf_mark=0, dx_target=0)

        def emit_xloads(s, t):
            P.wait("gpsimd", "pe", st["xs_guard"])
            base = T * t
            for c in range(4):
                P.op("gpsimd",
                     lambda g, c_=1024 * c, t0=base + 128 * c, s_=s:
                     g.dma_start(out=XS[0:128, c_:c_ + 1024],
                                 in_=xin[s_, t0:t0 + 128, :]),
                     inc="dx", n=16)
            st["dx_target"] = P.mark("dx")

        def emit_tile(s, t, nxt):
            base = T * t

            for g in range(12):
                ws.push(scr["qkv"], g)
            for g in range(4):
                ws.push(scr["proj"], g)
            for g in range(4):
                ws.push(scr["pw"], g)
            for g in range(32):
                ws.push(scr["exp"], g)
            for g in range(4):
                ws.push(scr["out"], g)
            ws.push(scr["ffd"], 0)
            ws.push(scr["ffu"], 0)
            ws.prime()

            def xload(cols, tok0, ntok, poff=0):
                P.op("gpsimd",
                     lambda g, c=cols, t0=tok0, nt_=ntok, po=poff, s_=s:
                     g.dma_start(out=XS[po:po + nt_, c:c + 1024],
                                 in_=xin[s_, t0:t0 + nt_, :]),
                     inc="dx", n=16)

            P.wait("gpsimd", "dy0", st["ytm_inflight"][0])
            t4 = (base + 512) % N
            P.op("gpsimd",
                 lambda g, t0=t4, s_=s: g.dma_start(out=VTY[0:4, :],
                                                    in_=xin[s_, t0:t0 + 4, :]),
                 inc="dx", n=16)
            dx_target = P.mark("dx")

            P.wait("tensor", "dx", max(dx_target, st["dx_target"]))

            def xpose_group(cols_in, np_, dstcol, bank_i, src_t=None):
                src_t = XS if src_t is None else src_t
                for g in range(2):
                    bk = banks[bank_i + g]
                    bk.acquire()
                    for dd in range(4):
                        d = 4 * g + dd
                        tp(pb[bank_i + g][0:128, dd * 128:dd * 128 + np_],
                           src_t[0:np_, cols_in + 128 * d:cols_in + 128 * (d + 1)],
                           IDN[0:np_, 0:np_], inc=(dd == 3))
                    bk.written()
                    bk.rwait("vector")
                    dsl = XRF[:].rearrange("p (c f) -> p c f", c=CH)[:, 4 * g:4 * g + 4,
                                                                    dstcol:dstcol + np_]
                    ssl = pb[bank_i + g][:].rearrange("p (c f) -> p c f", c=4)[:, :, 0:np_]
                    vop(lambda v, d_=dsl, s_=ssl: v.tensor_copy(d_, s_))
                    bk.radd("dve")

            for c in range(4):
                xpose_group(1024 * c, 128, 128 * c, (2 * c) % 8)
            xpose_group(0, 4, 512, 0, src_t=VTY)
            if t == 0:
                P.wait("gpsimd", "pe", P.mark("pe"))
                xload(0, N - 4, 4, poff=0)
                xload(0, 0, 4, poff=4)
                P.wait("tensor", "dx", P.mark("dx"))
                xpose_group(0, 8, 516, 2)
            st["xs_guard"] = P.mark("pe")
            st["xrf_mark"] = P.mark("dve")
            tap("xrf", XRF[:], "dve", st["xrf_mark"])
            if nxt is not None:
                emit_xloads(*nxt)
            if UPTO <= 1:
                return

            P.wait("vector", "dve", st["xrf_mark"])
            wwid = 524 if t == 0 else 516
            sp_wave([(XRF[:, XR_C * c:XR_C * c + wwid],
                      XRFh[:, XR_C * c:XR_C * c + wwid],
                      XRFl[:, XR_C * c:XR_C * c + wwid]) for c in range(CH)])
            xrhl_mark = P.mark("dve")
            P.wait("tensor", "dve", xrhl_mark)
            slot = None
            for m in range(24):
                if m % 2 == 0:
                    if slot is not None:
                        ws.done(slot)
                    slot = ws.consume()
                if m < 8:
                    dsth = QFh[:, QK_C * m:QK_C * m + 520]
                    dstl = QFl[:, QK_C * m:QK_C * m + 520]
                    ev = lambda v, d, p, c: v.tensor_scalar(
                        d, p, BQ[:, c:c + 1], 0.125, ALU.add, ALU.mult)
                elif m < 16:
                    dsth = KFh[:, QK_C * (m - 8):QK_C * (m - 8) + 520]
                    dstl = KFl[:, QK_C * (m - 8):QK_C * (m - 8) + 520]
                    ev = lambda v, d, p, c: v.tensor_scalar(
                        d, p, BQ[:, c:c + 1], None, ALU.add)
                else:
                    dsth = None
                    dstl = VF[:, QK_C * (m - 16):QK_C * (m - 16) + 520]
                    ev = lambda v, d, p, c: v.tensor_scalar(
                        d, p, BQ[:, c:c + 1], None, ALU.add)
                bk = banks[m % 4]
                bk.acquire()
                mm3(pb[m % 4][:], lambda hl, k, s_=slot: ws.block(s_, hl, m % 2, k, 8),
                    lambda k: XRFh[:, XR_C * k + 4:XR_C * k + 516],
                    lambda k: XRFl[:, XR_C * k + 4:XR_C * k + 516], CH)
                bk.written()
                bk.rwait("vector")
                if m >= 16:
                    vop(lambda v, d=dstl[:, 0:512], p=pb[m % 4], c=m, e_=ev:
                        e_(v, d, p[:], c))
                else:
                    sc_h = SC[:, 512 * (m % 2):512 * (m % 2) + 512]
                    vop(lambda v, d=sc_h, p=pb[m % 4], c=m, e_=ev: e_(v, d, p[:], c))
                    P.wait("vector", "dve", P.mark("dve"))
                    sp_evac(sc_h, dsth[:, 0:512], dstl[:, 0:512])
                bk.radd("dve")
                if t == 0:
                    bkw = banks[4 + m % 2]
                    bkw.acquire()
                    mm3(pb[4 + m % 2][0:128, 0:8],
                        lambda hl, k, s_=slot: ws.block(s_, hl, m % 2, k, 8),
                        lambda k: XRFh[:, XR_C * k + 516:XR_C * k + 524],
                        lambda k: XRFl[:, XR_C * k + 516:XR_C * k + 524], CH)
                    bkw.written()
                    bkw.rwait("vector")
                    if m >= 16:
                        vop(lambda v, d=dstl[:, 512:520], p=pb[4 + m % 2], c=m, e_=ev:
                            e_(v, d, p[0:128, 0:8], c))
                    else:
                        scw = SM[:, 56:64]
                        vop(lambda v, d=scw, p=pb[4 + m % 2], c=m, e_=ev:
                            e_(v, d, p[0:128, 0:8], c))
                        P.wait("vector", "dve", P.mark("dve"))
                        sp_evac(scw, dsth[:, 512:520], dstl[:, 512:520])
                    bkw.radd("dve")
            if slot is not None:
                ws.done(slot)

            qkv_mark = P.mark("dve")
            if UPTO <= 2:
                return
            tap("qf", QFh[:], "dve", qkv_mark)
            tap("kf", KFh[:], "dve", qkv_mark)
            tap("vf", VF[:], "dve", qkv_mark)

            P.wait("tensor", "dve", qkv_mark)

            def attn_block(qcol, ctx0, W):
                idw = IDN[0:W, 0:W]
                for g in range(2):
                    bk = banks[6 + g]
                    bk.acquire()
                    for dd in range(4):
                        d = 4 * g + dd
                        tp(pb[6 + g][0:W, dd * 128:(dd + 1) * 128],
                           VF[:, QK_C * d + qcol:QK_C * d + qcol + W],
                           IDN[:, :], inc=(dd == 3))
                    bk.written()
                    bk.rwait("vector")
                    sp_evac(pb[6 + g][0:W, :], VTh[0:W, 512 * g:512 * (g + 1)],
                            VTl[0:W, 512 * g:512 * (g + 1)])
                    bk.radd("dve")
                if ATTN_SUB <= 1:
                    return
                sbank = lambda h: 2 * (h % 2) + (h // 8)
                squar = lambda h: (h // 2) % 4
                exp_mark = [0, 0]
                rinv_mark = [0, 0]
                pts_mark = [0, 0]
                ctm_mark = [0, 0]

                def qsl(T_, h_):
                    return T_[64 * (h_ % 2):64 * (h_ % 2) + 64,
                              QK_C * (h_ // 2) + qcol:QK_C * (h_ // 2) + qcol + W]

                def emit_S(half):
                    hs = list(range(8 * half, 8 * half + 8))
                    for b_ in sorted(set(sbank(h) for h in hs)):
                        banks[b_].acquire()
                    for j, h in enumerate(hs):
                        tpos = (64, 0) if h % 2 else None
                        for i3, (lh, rh) in enumerate(((QFh, KFh), (QFl, KFh),
                                                       (QFh, KFl))):
                            mm(pb[sbank(h)][0:W, squar(h) * 128:squar(h) * 128 + W],
                               qsl(lh, h), qsl(rh, h), start=(i3 == 0), stop=(i3 == 2),
                               inc=(j == 7 and i3 == 2), tile_position=tpos)
                    for b_ in sorted(set(sbank(h) for h in hs)):
                        banks[b_].written()

                def emit_softmax(half):
                    hs = list(range(8 * half, 8 * half + 8))
                    for h in hs:
                        banks[sbank(h)].rwait("vector")
                        vop(lambda v, h_=h: v.tensor_tensor(
                            P0[0:W, 128 * h_:128 * h_ + W],
                            pb[sbank(h_)][0:W, squar(h_) * 128:squar(h_) * 128 + W],
                            MSK[0:W, 128 * h_:128 * h_ + W], ALU.add))
                        banks[sbank(h)].radd("dve")
                    P.wait("vector", "dve", P.mark("dve"))
                    for h in hs:
                        vop(lambda v, h_=h: v.tensor_reduce(
                            NS[0:W, h_:h_ + 1], P0[0:W, 128 * h_:128 * h_ + W],
                            axis=AX, op=ALU.max, negate=True))
                    P.wait("scalar", "dve", P.mark("dve"))
                    for h in hs:
                        aop(lambda sc, h_=h: sc.activation(
                            P0[0:W, 128 * h_:128 * h_ + W],
                            P0[0:W, 128 * h_:128 * h_ + W],
                            AF.Exp, bias=NS[0:W, h_:h_ + 1],
                            accum_out=NS[0:W, 16 + h_:17 + h_]))
                    exp_mark[half] = P.mark("act")
                    P.wait("vector", "act", exp_mark[half])
                    vop(lambda v, hf=half: v.reciprocal(
                        NS[0:W, 32 + 8 * hf:40 + 8 * hf],
                        NS[0:W, 16 + 8 * hf:24 + 8 * hf]))
                    rinv_mark[half] = P.mark("dve")

                def emit_PT(half):
                    hs = list(range(8 * half, 8 * half + 8))
                    base = 4 + 2 * half
                    P.wait("tensor", "act", exp_mark[half])
                    for j, h in enumerate(hs):
                        bk = banks[base + j // 4]
                        if j % 4 == 0:
                            bk.acquire()
                        tp(pb[base + j // 4][0:W, (j % 4) * 128:(j % 4) * 128 + W],
                           P0[0:W, 128 * h:128 * h + W], idw, inc=(j % 4 == 3))
                        if j % 4 == 3:
                            bk.written()
                    for g in range(2):
                        banks[base + g].rwait("vector")
                        gg = 2 * half + g
                        sp_evac(pb[base + g][0:W, :], PTSh[0:W, 512 * gg:512 * (gg + 1)],
                                PTSl[0:W, 512 * gg:512 * (gg + 1)])
                        banks[base + g].radd("dve")
                    pts_mark[half] = P.mark("dve")

                def emit_ctx(half):
                    hs = list(range(8 * half, 8 * half + 8))
                    P.wait("tensor", "dve", pts_mark[half])
                    bk = banks[half]
                    bk.acquire()
                    for j, h in enumerate(hs):
                        for i3, (lh, rh) in enumerate(((PTSh, VTh), (PTSl, VTh),
                                                       (PTSh, VTl))):
                            mm(pb[half][0:W, 64 * j:64 * j + 64],
                               lh[0:W, 128 * h:128 * h + W], rh[0:W, 64 * h:64 * h + 64],
                               start=(i3 == 0), stop=(i3 == 2),
                               inc=(j == 7 and i3 == 2))
                    bk.written()

                def emit_ctx_evac(half):
                    hs = list(range(8 * half, 8 * half + 8))
                    P.wait("vector", "dve", rinv_mark[half])
                    banks[half].rwait("vector")
                    for j, h in enumerate(hs):
                        vop(lambda v, h_=h, j_=j, hf=half: v.tensor_scalar(
                            VT[0:W, 64 * h_:64 * h_ + 64],
                            pb[hf][0:W, 64 * j_:64 * j_ + 64],
                            NS[0:W, 32 + h_:33 + h_], None, ALU.mult))
                    banks[half].radd("dve")
                    ctm_mark[half] = P.mark("dve")

                emit_S(0)
                emit_softmax(0)
                emit_S(1)
                emit_softmax(1)
                if ATTN_SUB <= 3:
                    return
                emit_PT(0)
                emit_PT(1)
                if ATTN_SUB <= 4:
                    return
                emit_ctx(0)
                emit_ctx(1)
                if ATTN_SUB <= 5:
                    return
                emit_ctx_evac(0)
                emit_ctx_evac(1)
                if ATTN_SUB <= 6:
                    return
                P.wait("tensor", "dve", ctm_mark[1])
                for g in range(2):
                    bk = banks[2 + g]
                    bk.acquire()
                    for cc_ in range(4):
                        c = 4 * g + cc_
                        tp(pb[2 + g][0:128, cc_ * 128:cc_ * 128 + W],
                           VT[0:W, 128 * c:128 * (c + 1)], idw, inc=(cc_ == 3))
                    bk.written()
                    bk.rwait("vector")
                    dslh = CTXFh[:].rearrange("p (c f) -> p c f", c=CH)[
                        :, 4 * g:4 * g + 4, ctx0:ctx0 + W]
                    dsll = CTXFl[:].rearrange("p (c f) -> p c f", c=CH)[
                        :, 4 * g:4 * g + 4, ctx0:ctx0 + W]
                    ssl = pb[2 + g][:].rearrange("p (c f) -> p c f", c=4)[:, :, 0:W]
                    sp_evac(ssl, dslh, dsll)
                    bk.radd("dve")

            P.wait("vector", "pe", st["ctxf_free"])
            P.wait("vector", "dy0", st["ytm_inflight"][0])
            P.wait("vector", "dy1", st["ytm_inflight"][1])
            if t == 0:
                if ATTN_PARTS is None or "mini" in ATTN_PARTS:
                    attn_block(512, 0, 8)
            else:
                if ATTN_PARTS is None or "tail" in ATTN_PARTS:
                    P.wait("vector", "dve", st["ctail_mark"])
                    for Td, Ts in ((CTXFh, CTAILh), (CTXFl, CTAILl)):
                        dsl = Td[:].rearrange("p (c f) -> p c f", c=CH)[:, :, 0:8]
                        ssl = Ts[:].rearrange("p (c f) -> p c f", c=CH)
                        vop(lambda v, d_=dsl, s_=ssl: v.tensor_copy(d_, s_))
            for b in range(NB):
                if ATTN_PARTS is None or ("b%d" % b) in ATTN_PARTS:
                    attn_block(128 * b, 8 + 128 * b, 128)
            for Td, Ts in ((CTAILh, CTXFh), (CTAILl, CTXFl)):
                dsl = Td[:].rearrange("p (c f) -> p c f", c=CH)
                ssl = Ts[:].rearrange("p (c f) -> p c f", c=CH)[:, :, 512:520]
                vop(lambda v, d_=dsl, s_=ssl: v.tensor_copy(d_, s_))
            st["ctail_mark"] = P.mark("dve")
            ctx_mark = P.mark("dve")
            tap("ctx", CTXFh[:], "dve", ctx_mark)
            if UPTO <= 3:
                return

            P.wait("tensor", "dve", ctx_mark)
            for g in range(4):
                slot = ws.consume()
                for ml in range(2):
                    m = 2 * g + ml
                    bk = banks[m % 4]
                    bk.acquire()
                    mm3(pb[m % 4][:],
                        lambda hl, k, s_=slot, ml_=ml: ws.block(s_, hl, ml_, k, 8),
                        lambda k: CTXFh[:, CT_C * k + 4:CT_C * k + 516],
                        lambda k: CTXFl[:, CT_C * k + 4:CT_C * k + 516], CH)
                    bk.written()
                    bk.rwait("vector")
                    sc_h = SC[:, 512 * (m % 2):512 * (m % 2) + 512]
                    vop(lambda v, d=sc_h, p=pb[m % 4], c=m:
                        v.tensor_scalar(d, p[:], BP[:, c:c + 1], None, ALU.add))
                    bk.radd("dve")
                    P.wait("vector", "dve", P.mark("dve"))
                    vop(lambda v, d=VF[:, QK_C * m:QK_C * m + 512], s_=sc_h, c=m:
                        v.tensor_tensor(d, s_, XRF[:, XR_C * c:XR_C * c + 512], ALU.add))
                ws.done(slot)
            rf_mark = P.mark("dve")
            tap("rf", VF[:], "dve", rf_mark)
            if UPTO <= 4:
                return

            h_dst = S1B[:].rearrange("p (c f) -> p c f", c=CH)[:, :, 0:2]
            if t == 0:
                vop(lambda v, d=h_dst: v.memset(d, 0.0))
            else:
                h_src = S1B[:].rearrange("p (c f) -> p c f", c=CH)[:, :, 512:514]
                vop(lambda v, d=h_dst, s_=h_src: v.tensor_copy(d, s_))
            halo_mark = P.mark("dve")

            def emit_norm(src_stride, src_t, dst_write, eps, subtract_mean,
                          read_guard_sem=None, read_guard=0):
                if read_guard_sem is not None:
                    P.wait("scalar", read_guard_sem, read_guard)
                    P.wait("tensor", read_guard_sem, read_guard)
                P.wait("scalar", "pe", P.mark("pe"))
                P.wait("scalar", "dve", P.mark("dve"))
                bkA, bkB = banks[4], banks[5]
                bkA.acquire()
                bkB.acquire()
                if subtract_mean:
                    for k in range(CH):
                        mm(pb[4][0:1, :], OC[:], src_t[:, src_stride * k:src_stride * k + 512],
                           start=(k == 0), stop=(k == CH - 1), inc=(k == CH - 1))
                bkA.written()
                half_guard = [0, 0]
                for c in range(CH):
                    if half_guard[c % 2]:
                        P.wait("scalar", "pe", half_guard[c % 2])
                    aop(lambda sc, c_=c: sc.activation(
                        SC[:, 512 * (c_ % 2):512 * (c_ % 2) + 512],
                        src_t[:, src_stride * c_:src_stride * c_ + 512], AF.Square))
                    P.wait("tensor", "act", P.mark("act"))
                    mm(pb[5][0:1, :], OC[:], SC[:, 512 * (c % 2):512 * (c % 2) + 512],
                       start=(c == 0), stop=(c == CH - 1), inc=True)
                    half_guard[c % 2] = P.mark("pe")
                bkB.written()
                seg = lambda i: STATS[0:1, 512 * i:512 * (i + 1)]
                bkB.rwait("vector")
                if subtract_mean:
                    bkA.rwait("vector")
                    vop(lambda v: v.tensor_scalar(seg(0), pb[4][0:1, :], 1.0 / D, None, ALU.mult))
                    bkA.radd("dve")
                vop(lambda v: v.tensor_scalar(seg(1), pb[5][0:1, :], 1.0 / D, None, ALU.mult))
                bkB.radd("dve")
                P.wait("vector", "dve", P.mark("dve"))
                if subtract_mean:
                    vop(lambda v: v.tensor_tensor(seg(2), seg(0), seg(0), ALU.mult))
                    P.wait("vector", "dve", P.mark("dve"))
                    vop(lambda v: v.tensor_tensor(seg(1), seg(1), seg(2), ALU.subtract))
                    P.wait("vector", "dve", P.mark("dve"))
                vop(lambda v, e=eps: v.tensor_scalar(seg(1), seg(1), 1.0, e, ALU.mult, ALU.add))
                vm = P.mark("dve")
                P.wait("scalar", "dve", vm)
                aop(lambda sc: sc.activation(seg(2), seg(1), AF.Ln))
                P.wait("scalar", "act", P.mark("act"))
                aop(lambda sc: sc.activation(seg(2), seg(2), AF.Exp, scale=-0.5))
                rim = P.mark("act")
                if subtract_mean:
                    P.wait("vector", "act", rim)
                    vop(lambda v: v.tensor_tensor(seg(3), seg(0), seg(2), ALU.mult))
                    P.wait("vector", "dve", P.mark("dve"))
                    vop(lambda v: v.tensor_scalar(seg(3), seg(3), -1.0, None, ALU.mult))
                nmr_mark = P.mark("dve")
                bkC, bkD = banks[6], banks[7]
                bkC.acquire()
                P.wait("tensor", "act", rim)
                mm(pb[6][:], OR[:], seg(2), start=True, stop=True, inc=True)
                bkC.written()
                if subtract_mean:
                    bkD.acquire()
                    P.wait("tensor", "dve", nmr_mark)
                    mm(pb[7][:], OR[:], seg(3), start=True, stop=True, inc=True)
                    bkD.written()
                bkC.rwait("vector")
                if subtract_mean:
                    bkD.rwait("vector")
                dst_write(bkC, bkD)
                bkC.radd("dve")
                if subtract_mean:
                    bkD.radd("dve")

            def ln1_write(bkC, bkD):
                P.wait("vector", "dve", halo_mark)
                for c in range(CH):
                    vop(lambda v, c_=c: v.tensor_tensor(
                        S1B[:, S1_C * c_ + 2:S1_C * c_ + 514],
                        VF[:, QK_C * c_:QK_C * c_ + 512], pb[6][:], ALU.mult))
                P.wait("vector", "dve", P.mark("dve"))
                for c in range(CH):
                    vop(lambda v, c_=c: v.tensor_tensor(
                        S1B[:, S1_C * c_ + 2:S1_C * c_ + 514],
                        S1B[:, S1_C * c_ + 2:S1_C * c_ + 514], pb[7][:], ALU.add))

            P.wait("vector", "dve", rf_mark)
            emit_norm(QK_C, VF, ln1_write, EPS_LN, True,
                      read_guard_sem="dve", read_guard=rf_mark)
            s1_mark = P.mark("dve")
            tap("s1", S1B[:], "dve", s1_mark)
            if UPTO <= 5:
                return

            P.wait("vector", "pe", P.mark("pe"))
            P.wait("vector", "dve", s1_mark)
            for c in range(CH):
                vop(lambda v, c_=c: v.tensor_scalar(
                    VF[:, QK_C * c_:QK_C * c_ + 512],
                    S1B[:, S1_C * c_ + 2:S1_C * c_ + 514],
                    DWK[:, 3 * c_ + 2:3 * c_ + 3], DWB[:, c_:c_ + 1], ALU.mult, ALU.add))
            for tap_i, off in ((1, 1), (0, 0)):
                for hf in range(2):
                    cs = range(4 * hf, 4 * hf + 4)
                    for c in cs:
                        vop(lambda v, c_=c, o=off, ti=tap_i: v.tensor_scalar(
                            P0[:, 512 * (c_ % 4):512 * (c_ % 4) + 512],
                            S1B[:, S1_C * c_ + o:S1_C * c_ + o + 512],
                            DWK[:, 3 * c_ + ti:3 * c_ + ti + 1], None, ALU.mult))
                    P.wait("vector", "dve", P.mark("dve"))
                    for c in cs:
                        vop(lambda v, c_=c: v.tensor_tensor(
                            VF[:, QK_C * c_:QK_C * c_ + 512],
                            VF[:, QK_C * c_:QK_C * c_ + 512],
                            P0[:, 512 * (c_ % 4):512 * (c_ % 4) + 512], ALU.add))
                    P.wait("vector", "dve", P.mark("dve"))
            sp_wave([(VF[:, QK_C * c:QK_C * c + 512],
                      QFh[:, QK_C * c:QK_C * c + 512],
                      QFl[:, QK_C * c:QK_C * c + 512]) for c in range(CH)])
            cv_mark = P.mark("dve")
            tap("cv", VF[:], "dve", cv_mark)
            if UPTO <= 6:
                return

            P.wait("tensor", "dve", cv_mark)
            for g in range(4):
                slot = ws.consume()
                for ml in range(2):
                    m = 2 * g + ml
                    bk = banks[m % 4]
                    bk.acquire()
                    mm3(pb[m % 4][:],
                        lambda hl, k, s_=slot, ml_=ml: ws.block(s_, hl, ml_, k, 8),
                        lambda k: QFh[:, QK_C * k:QK_C * k + 512],
                        lambda k: QFl[:, QK_C * k:QK_C * k + 512], CH)
                    bk.written()
                    bk.rwait("scalar")
                    sc_h = SC[:, 512 * (m % 2):512 * (m % 2) + 512]
                    aop(lambda sc, d=sc_h, p=pb[m % 4], c=m:
                        sc.activation(d, p[:], AF.Silu, bias=PWB[:, c:c + 1]))
                    bk.radd("act")
                    P.wait("vector", "act", P.mark("act"))
                    vop(lambda v, c_=m, s_=sc_h: v.tensor_tensor(
                        XRF[:, XR_C * c_:XR_C * c_ + 512],
                        S1B[:, S1_C * c_ + 2:S1_C * c_ + 514], s_, ALU.add))
                ws.done(slot)
            P.wait("vector", "dve", P.mark("dve"))
            sp_wave([(XRF[:, XR_C * c:XR_C * c + 512],
                      XRFh[:, XR_C * c:XR_C * c + 512],
                      XRFl[:, XR_C * c:XR_C * c + 512]) for c in range(CH)])
            x1_mark = P.mark("dve")
            tap("x1", XRF[:], "dve", x1_mark)
            if UPTO <= 7:
                return

            P.wait("tensor", "dve", x1_mark)
            for mt in range(4):
                bk = banks[4 + mt]
                bk.acquire()
                for k in range(CH):
                    mm(pb[4 + mt][0:128, 0:8],
                       XRF[:, XR_C * k + 128 * mt:XR_C * k + 128 * (mt + 1)],
                       RT[:, 8 * k:8 * k + 8],
                       start=(k == 0), stop=(k == CH - 1), inc=(k == CH - 1))
                bk.written()
                bk.rwait("vector")
                vop(lambda v, m_=mt: v.tensor_tensor(
                    LG[:, 8 * m_:8 * m_ + 8], pb[4 + m_][0:128, 0:8], RBB[:], ALU.add))
                bk.radd("dve")
            col = lambda t_, i: t_[:, i:i + 1]
            for mt in range(4):
                lg = LG[:, 8 * mt:8 * mt + 8]
                m0c, m1c = col(SM, mt), col(SM, 4 + mt)
                w0c, w1c = col(SM, 8 + mt), col(SM, 12 + mt)
                ec, e1c = col(SM, 16 + mt), col(SM, 20 + mt)
                eq0 = SM[:, 24 + 8 * mt:32 + 8 * mt]
                eq1 = RWV[:, 8 * mt:8 * mt + 8]

                def step(f):
                    vop(f)
                    P.wait("vector", "dve", P.mark("dve"))
                step(lambda v, a=m0c, b=lg: v.tensor_reduce(a, b, axis=AX, op=ALU.max))
                step(lambda v, a=eq0, b=lg, c=m0c: v.tensor_scalar(a, b, c, None, ALU.is_ge))
                step(lambda v, a=eq1, b=eq0: v.tensor_scalar(a, b, -1e30, None, ALU.mult))
                step(lambda v, a=eq1, b=lg: v.tensor_tensor(a, a, b, ALU.add))
                step(lambda v, a=m1c, b=eq1: v.tensor_reduce(a, b, axis=AX, op=ALU.max))
                step(lambda v, a=eq1, b=lg, c=m1c: v.tensor_scalar(a, b, c, None, ALU.is_ge))
                step(lambda v, a=eq1, b=eq0: v.tensor_tensor(a, a, b, ALU.subtract))
                step(lambda v, a=ec, b=m1c, c=m0c: v.tensor_tensor(a, b, c, ALU.subtract))
                P.wait("scalar", "dve", P.mark("dve"))
                aop(lambda sc, a=ec: sc.activation(a, a, AF.Exp))
                P.wait("vector", "act", P.mark("act"))
                step(lambda v, a=e1c, b=ec: v.tensor_scalar(a, b, 1.0, 1.0, ALU.mult, ALU.add))
                step(lambda v, a=w1c, b=e1c: v.reciprocal(a, b))
                step(lambda v, a=w1c, b=ec: v.tensor_tensor(a, a, b, ALU.mult))
                step(lambda v, a=w0c, b=w1c: v.tensor_scalar(a, b, -1.0, 1.0, ALU.mult, ALU.add))
                step(lambda v, a=eq0, b=w0c: v.tensor_scalar(a, a, b, None, ALU.mult))
                step(lambda v, a=eq1, b=w1c: v.tensor_scalar(a, a, b, None, ALU.mult))
                step(lambda v, a=eq1, b=eq0: v.tensor_tensor(a, a, b, ALU.add))
            rw_mark = P.mark("dve")
            tap("lg", LG[:], "dve", rw_mark)
            tap("rw", RWV[:], "dve", rw_mark)
            P.wait("tensor", "dve", rw_mark)
            bk = banks[4]
            bk.acquire()
            for mt in range(4):
                tp(pb[4][0:8, 128 * mt:128 * (mt + 1)], RWV[:, 8 * mt:8 * mt + 8],
                   IDN[:, :], inc=(mt == 3))
            bk.written()
            bk.rwait("vector")
            vop(lambda v: v.tensor_copy(RWF8[0:8, :], pb[4][0:8, 0:512]))
            bk.radd("dve")
            rwf_mark = P.mark("dve")
            if UPTO <= 8:
                return

            P.wait("tensor", "dve", rwf_mark)
            for e in range(E):
                rwbc_bank = 6 + e % 2
                nb_ = banks[rwbc_bank]
                nb_.acquire()
                mm(pb[rwbc_bank][:], SEL[0:8, 128 * e:128 * (e + 1)], RWF8[0:8, :],
                   start=True, stop=True, inc=True)
                nb_.written()
                for m in range(CH):
                    gi = e * 8 + m
                    if gi % 2 == 0:
                        slot = ws.consume()
                    bk = banks[gi % 4]
                    bk.acquire()
                    mm3(pb[gi % 4][:],
                        lambda hl, k, s_=slot, ml_=gi % 2: ws.block(s_, hl, ml_, k, 8),
                        lambda k: XRFh[:, XR_C * k:XR_C * k + 512],
                        lambda k: XRFl[:, XR_C * k:XR_C * k + 512], CH)
                    bk.written()
                    bk.rwait("scalar")
                    sc_h = SC[:, 512 * (m % 2):512 * (m % 2) + 512]
                    aop(lambda sc, d=sc_h, p=pb[gi % 4], e_=e, m_=m:
                        sc.activation(d, p[:], AF.Silu, bias=EB[:, 8 * e_ + m_:8 * e_ + m_ + 1]))
                    bk.radd("act")
                    am = P.mark("act")
                    P.wait("vector", "act", am)
                    banks[rwbc_bank].rwait("vector")
                    if e == 0:
                        vop(lambda v, d=COMB[:, 512 * m:512 * m + 512], s_=sc_h,
                            rb=pb[rwbc_bank]: v.tensor_tensor(d, s_, rb[:], ALU.mult))
                    else:
                        vop(lambda v, s_=sc_h, rb=pb[rwbc_bank], m_=m:
                            v.tensor_tensor(P0[:, 512 * (m_ % 2):512 * (m_ % 2) + 512],
                                            s_, rb[:], ALU.mult))
                        P.wait("vector", "dve", P.mark("dve"))
                        vop(lambda v, d=COMB[:, 512 * m:512 * m + 512], m_=m:
                            v.tensor_tensor(d, d, P0[:, 512 * (m_ % 2):512 * (m_ % 2) + 512],
                                            ALU.add))
                    banks[rwbc_bank].radd("dve")
                    if gi % 2 == 1:
                        ws.done(slot)
            P.wait("vector", "pe", P.mark("pe"))
            P.wait("vector", "dve", P.mark("dve"))
            sp_wave([(COMB[:, 512 * c:512 * c + 512],
                      KFh[:, QK_C * c:QK_C * c + 512],
                      KFl[:, QK_C * c:QK_C * c + 512]) for c in range(CH)])
            comb_mark = P.mark("dve")
            tap("comb", COMB[:], "dve", comb_mark)
            if UPTO <= 9:
                return

            P.wait("tensor", "dve", comb_mark)
            for g in range(4):
                slot = ws.consume()
                for ml in range(2):
                    m = 2 * g + ml
                    bk = banks[m % 4]
                    bk.acquire()
                    mm3(pb[m % 4][:],
                        lambda hl, k, s_=slot, ml_=ml: ws.block(s_, hl, ml_, k, 8),
                        lambda k: KFh[:, QK_C * k:QK_C * k + 512],
                        lambda k: KFl[:, QK_C * k:QK_C * k + 512], CH)
                    bk.written()
                    bk.rwait("vector")
                    sc_h = SC[:, 512 * (m % 2):512 * (m % 2) + 512]
                    vop(lambda v, d=sc_h, p=pb[m % 4], c=m:
                        v.tensor_scalar(d, p[:], OB[:, c:c + 1], None, ALU.add))
                    bk.radd("dve")
                    P.wait("vector", "dve", P.mark("dve"))
                    vop(lambda v, c=m, s_=sc_h: v.tensor_tensor(
                        XRF[:, XR_C * c:XR_C * c + 512],
                        XRF[:, XR_C * c:XR_C * c + 512], s_, ALU.add))
                ws.done(slot)
            hp_mark = P.mark("dve")
            tap("hp", XRF[:], "dve", hp_mark)
            if UPTO <= 10:
                return

            def rms_write(bkC, bkD):
                for c in range(CH):
                    vop(lambda v, c_=c: v.tensor_tensor(
                        VF[:, QK_C * c_:QK_C * c_ + 512],
                        XRF[:, XR_C * c_:XR_C * c_ + 512], pb[6][:], ALU.mult))
                P.wait("vector", "dve", P.mark("dve"))
                for c in range(CH):
                    vop(lambda v, c_=c: v.tensor_scalar(
                        VF[:, QK_C * c_:QK_C * c_ + 512],
                        VF[:, QK_C * c_:QK_C * c_ + 512],
                        NW[:, c_:c_ + 1], None, ALU.mult))

            P.wait("vector", "dve", hp_mark)
            P.wait("vector", "pe", P.mark("pe"))
            emit_norm(XR_C, XRF, rms_write, EPS_RMS, False,
                      read_guard_sem="dve", read_guard=hp_mark)
            P.wait("vector", "dve", P.mark("dve"))
            sp_wave([(VF[:, QK_C * c:QK_C * c + 512],
                      CTXFh[:, CT_C * c:CT_C * c + 512],
                      CTXFl[:, CT_C * c:CT_C * c + 512]) for c in range(CH)])
            h_mark = P.mark("dve")
            tap("h", VF[:], "dve", h_mark)
            if UPTO <= 11:
                return

            P.wait("tensor", "dve", h_mark)
            slot = ws.consume()
            for m in range(2):
                bk = banks[m]
                bk.acquire()
                mm3(pb[m][:],
                    lambda hl, k, s_=slot, m_=m: ws.block(s_, hl, m_, k, 8),
                    lambda k: CTXFh[:, CT_C * k:CT_C * k + 512],
                    lambda k: CTXFl[:, CT_C * k:CT_C * k + 512], CH)
                bk.written()
                bk.rwait("scalar")
                aop(lambda sc, d=FF32[:, 512 * m:512 * m + 512], p=pb[m], c=m:
                    sc.activation(d, p[:], AF.Gelu, bias=FDB[:, c:c + 1]))
                bk.radd("act")
            ws.done(slot)
            st["ctxf_free"] = P.mark("pe")
            ff1_mark = P.mark("act")
            P.wait("vector", "act", ff1_mark)
            P.wait("vector", "pe", P.mark("pe"))
            sp_wave([(FF32[:, 512 * m:512 * m + 512],
                      VTh[:, 512 * m:512 * m + 512],
                      VTl[:, 512 * m:512 * m + 512]) for m in range(2)])
            ff1_mark = P.mark("dve")
            tap("ff1", FF32[:], "dve", ff1_mark)
            if UPTO <= 12:
                return

            P.wait("tensor", "dve", ff1_mark)
            slot = ws.consume()
            for m in range(CH):
                bk = banks[m % 4]
                bk.acquire()
                mm3(pb[m % 4][:],
                    lambda hl, k, s_=slot, m_=m: ws.block(s_, hl, m_, k, 2),
                    lambda k: VTh[:, 512 * k:512 * k + 512],
                    lambda k: VTl[:, 512 * k:512 * k + 512], 2)
                bk.written()
                bk.rwait("vector")
                sc_h = SC[:, 512 * (m % 2):512 * (m % 2) + 512]
                vop(lambda v, d=sc_h, p=pb[m % 4], c=m:
                    v.tensor_scalar(d, p[:], FUB[:, c:c + 1], None, ALU.add))
                bk.radd("dve")
                P.wait("vector", "dve", P.mark("dve"))
                vop(lambda v, c=m, s_=sc_h: v.tensor_tensor(
                    XRF[:, XR_C * c:XR_C * c + 512], s_,
                    VF[:, QK_C * c:QK_C * c + 512], ALU.add))
                P.wait("vector", "dve", P.mark("dve"))
                vop(lambda v, c=m: v.tensor_tensor(
                    XRF[:, XR_C * c:XR_C * c + 512],
                    XRF[:, XR_C * c:XR_C * c + 512],
                    S1B[:, S1_C * c + 2:S1_C * c + 514], ALU.add))
            ws.done(slot)
            yp_mark = P.mark("dve")
            tap("yp", XRF[:], "dve", yp_mark)
            if UPTO <= 13:
                return

            def ln3_write(bkC, bkD):
                for c in range(CH):
                    vop(lambda v, c_=c: v.tensor_tensor(
                        VF[:, QK_C * c_:QK_C * c_ + 512],
                        XRF[:, XR_C * c_:XR_C * c_ + 512], pb[6][:], ALU.mult))
                P.wait("vector", "dve", P.mark("dve"))
                for c in range(CH):
                    vop(lambda v, c_=c: v.tensor_tensor(
                        VF[:, QK_C * c_:QK_C * c_ + 512],
                        VF[:, QK_C * c_:QK_C * c_ + 512], pb[7][:], ALU.add))

            P.wait("vector", "dve", yp_mark)
            emit_norm(XR_C, XRF, ln3_write, EPS_LN, True,
                      read_guard_sem="dve", read_guard=yp_mark)
            yf_mark = P.mark("dve")
            tap("yf", VF[:], "dve", yf_mark)
            if UPTO <= 14:
                return

            P.wait("tensor", "dve", yf_mark)
            for mt in range(4):
                for g in range(2):
                    bk = banks[2 * (mt % 2) + g]
                    bk.acquire()
                    for dd in range(4):
                        d = 4 * g + dd
                        tp(pb[2 * (mt % 2) + g][0:128, dd * 128:(dd + 1) * 128],
                           VF[:, QK_C * d + 128 * mt:QK_C * d + 128 * (mt + 1)],
                           IDN[:, :], inc=(dd == 3))
                    bk.written()
                    bk.rwait("vector")
                    P.wait("vector", "dy0", st["ytm_inflight"][0])
                    vop(lambda v, g_=g, b_=2 * (mt % 2) + g:
                        v.tensor_copy(VTY[:, 512 * g_:512 * (g_ + 1)], pb[b_][:]))
                    bk.radd("dve")
                em = P.mark("dve")
                P.wait("gpsimd", "dve", em)
                P.op("gpsimd",
                     lambda g_, s_=s, o=base + 128 * mt:
                     g_.dma_start(out=y[s_, o:o + 128, :], in_=VTY[:]),
                     inc="dy0", n=16)
                st["ytm_inflight"][0] = P.mark("dy0")

        ntiles = SEQS * NT if LIMIT_TILES is None else LIMIT_TILES
        seq = [(s, t) for s in range(SEQS) for t in range(NT)][:ntiles]
        emit_xloads(*seq[0])
        for i, (s, t) in enumerate(seq):
            emit_tile(s, t, seq[i + 1] if i + 1 < len(seq) else None)

        P.wait("gpsimd", "dy0", P.mark("dy0"))
        P.wait("gpsimd", "dy1", P.mark("dy1"))
        if taps:
            P.wait("gpsimd", "ddbg", P.mark("ddbg"))

        P.check_deadlock()
        with nc.Block() as block:
            P.replay(block)

    return nc


def _prep(inputs):
    g = {k: np.asarray(v) for k, v in inputs.items()}

    def blk(w, nm, nk):
        return np.ascontiguousarray(
            w.reshape(nk, 128, nm, 128).transpose(2, 0, 1, 3).astype(np.float32))

    out = {}
    out["wqkv_b"] = blk(g["L0_attn_w_qkv"], 24, 8)
    out["wproj_b"] = blk(g["L0_attn_w_proj"], 8, 8)
    out["wpw_b"] = blk(np.ascontiguousarray(g["L0_ffn_pw_w"].T), 8, 8)
    out["wexp_b"] = np.ascontiguousarray(
        np.stack([blk(np.ascontiguousarray(g["L0_ffn_expert_w"][e].T), 8, 8)
                  for e in range(8)]).reshape(64, 8, 128, 128))
    out["wout_b"] = blk(np.ascontiguousarray(g["L0_ffn_out_w"].T), 8, 8)
    out["wffd_b"] = blk(np.ascontiguousarray(g["L0_ffn_ff_down_w"].T), 2, 8)
    out["wffu_b"] = blk(np.ascontiguousarray(g["L0_ffn_ff_up_w"].T), 8, 2)

    def pmaj(a, *shape):
        a = np.asarray(a, np.float32).reshape(*shape)
        return np.ascontiguousarray(np.moveaxis(a, -1, 0).reshape(128, -1))

    bt = g["L0_attn_bias_table"].astype(np.float32)
    idx = 112 + (np.arange(8)[:, None] - np.arange(8)[None, :])
    bias8 = bt[idx.reshape(-1), :].reshape(8, 8, NH).transpose(2, 0, 1)
    M = np.full((NH, 128, 128), -1e9, np.float32)
    for w in range(16):
        M[:, 8 * w:8 * w + 8, 8 * w:8 * w + 8] = bias8
    out["mask_c"] = np.ascontiguousarray(M.transpose(1, 0, 2).reshape(128, NH * 128))
    out["ident_c"] = np.eye(128, dtype=np.float32)
    out["onescol_c"] = np.ones((128, 1), np.float32)
    out["onesrow_c"] = np.ones((1, 128), np.float32)
    out["sel_c"] = np.ascontiguousarray(
        np.kron(np.eye(8, dtype=np.float32), np.ones((1, 128), np.float32)))
    out["bqkv_c"] = pmaj(g["L0_attn_b_qkv"], 24, 128)
    out["bproj_c"] = pmaj(g["L0_attn_b_proj"], 8, 128)
    out["pwb_c"] = pmaj(g["L0_ffn_pw_b"], 8, 128)
    out["outb_c"] = pmaj(g["L0_ffn_out_b"], 8, 128)
    out["ffub_c"] = pmaj(g["L0_ffn_ff_up_b"], 8, 128)
    out["ffdb_c"] = pmaj(g["L0_ffn_ff_down_b"], 2, 128)
    out["nw_c"] = pmaj(g["L0_ffn_norm_w"], 8, 128)
    out["dwb_c"] = pmaj(g["L0_ffn_dw_bias"], 8, 128)
    dwk = np.asarray(g["L0_ffn_dw_kernel"], np.float32).reshape(8, 128, 3)
    out["dwk_c"] = np.ascontiguousarray(dwk.transpose(1, 0, 2).reshape(128, 24))
    rt = np.asarray(g["L0_ffn_router_w"], np.float32).T.reshape(8, 128, 8)
    out["rt_c"] = np.ascontiguousarray(rt.transpose(1, 0, 2).reshape(128, 64))
    out["rbbc_c"] = np.ascontiguousarray(
        np.broadcast_to(g["L0_ffn_router_b"].astype(np.float32)[None, :], (128, 8)))
    eb = np.asarray(g["L0_ffn_expert_b"], np.float32).reshape(8, 8, 128)
    out["eb_c"] = np.ascontiguousarray(eb.transpose(2, 0, 1).reshape(128, 64))
    return out


def kernel(**inputs):
    nc = build_nc()
    shared = _prep(inputs)
    x = np.ascontiguousarray(np.asarray(inputs["L0_attn_x"], dtype=np.float32))
    in_maps = []
    for c in range(NCORES):
        m = dict(shared)
        m["xin"] = np.ascontiguousarray(x[SEQS * c:SEQS * (c + 1)])
        in_maps.append(m)
    res = run_bass_kernel_spmd(nc, in_maps, core_ids=list(range(NCORES)))
    return np.concatenate([r["y"] for r in res.results], axis=0)
